# version 08
# baseline (speedup 1.0000x reference)
import numpy as np
import ml_dtypes
from contextlib import ExitStack

import concourse.bass as bass
import concourse.mybir as mybir
from concourse.bass_utils import run_bass_kernel_spmd

F32 = mybir.dt.float32
BF16 = mybir.dt.bfloat16
AF = mybir.ActivationFunctionType
ALU = mybir.AluOpType

B, T, DM = 4, 8192, 1024
H, DH = 8, 128
NQ = 4096
NT = 16
NOWN = 8
MULT = float(DH) ** -0.5
EPS = 1e-5
NEG = -30000.0
C2 = [16, 16, 16, 16, 32, 32, 32, 32]
C2NEW = [16, 0, 0, 0, 16, 0, 0, 0]


def _own_oth(w):
    if w == 0:
        own = np.r_[0:2048, 6144:8192]
        oth = np.r_[2048:6144]
    else:
        own = np.r_[2048:6144]
        oth = np.r_[0:2048, 6144:8192]
    return own, oth


def p2_slot_pairs(i):
    pairs = []
    n_own = 4 * (i + 1)
    pidx = 0
    for p in range(n_own // 2):
        c0 = 2 * p
        pairs.append(("own", c0, pidx, c0 >= n_own - 4))
        pidx += 1
    for p in range(C2[i] // 2):
        pairs.append(("oth", 32 + 2 * p, pidx, False))
        pidx += 1
    return pairs


def p2_structure():
    pair_list, slot_meta = [], {}
    kvl_dma = 0
    for h in range(H):
        for i in range(8):
            n_dma = 2 + (3 if C2NEW[i] else 0)
            kvl_dma += n_dma
            pairs = p2_slot_pairs(i)
            first_gp = len(pair_list)
            for (kind, c0, pidx, masked) in pairs:
                pair_list.append(dict(
                    h=h, i=i, kind=kind, c0=c0, pidx=pidx, masked=masked,
                    gp=len(pair_list), first=(len(pair_list) == first_gp),
                    kvl_after=kvl_dma, last_of_slot=False))
            pair_list[-1]["last_of_slot"] = True
            slot_meta[(h, i)] = dict(first_gp=first_gp,
                                     last_gp=len(pair_list) - 1,
                                     n_slot=h * 8 + i + 1,
                                     kvl_after=kvl_dma)
    n_msk = 0
    for d in pair_list:
        if d["masked"]:
            n_msk += 1
        d["msk_after"] = n_msk
    return pair_list, slot_meta


P2_PAIRS, P2_SLOTS = p2_structure()
NSLOT = 64


def p1_gseq(t):
    seq = [("K", h) for h in range(H)]
    if t < NOWN:
        seq += [("Q", h) for h in range(H)]
    return seq


P1_GCUM = [0]
for _t in range(NT):
    P1_GCUM.append(P1_GCUM[-1] + len(p1_gseq(_t)))

P4_LAYERS = [("w1", 32, 8), ("w2", 32, 8), ("h1", 32, 32), ("h2", 32, 32)]
GR_TILE = 256


def build_program(dbg=False, races=True):
    nc = bass.Bass("TRN2", target_bir_lowering=False,
                   detect_race_conditions=races)
    SCR = "ExternalOutput" if dbg else "Internal"

    dx = nc.dram_tensor("x", [T, DM], F32, kind="ExternalInput").ap()
    dwq = nc.dram_tensor("wqb", [8, 128, 8, 128], F32, kind="ExternalInput").ap()
    dwk = nc.dram_tensor("wkb", [8, 128, 8, 128], F32, kind="ExternalInput").ap()
    dwv = nc.dram_tensor("wvb", [8, 128, 1024], F32, kind="ExternalInput").ap()
    dwo = nc.dram_tensor("wob", [128, 8, 1024], F32, kind="ExternalInput").ap()
    dw1 = nc.dram_tensor("w1b", [32, 128, 8, 128], F32, kind="ExternalInput").ap()
    dw2 = nc.dram_tensor("w2b", [32, 128, 8, 128], F32, kind="ExternalInput").ap()
    dh1 = nc.dram_tensor("h1b", [32, 128, 32, 128], F32, kind="ExternalInput").ap()
    dh2 = nc.dram_tensor("h2b", [32, 128, 32, 128], F32, kind="ExternalInput").ap()
    dw3 = nc.dram_tensor("w3b", [32, 128, 1024], F32, kind="ExternalInput").ap()
    dsct = nc.dram_tensor("scT", [128, 128], F32, kind="ExternalInput").ap()
    dw3s = nc.dram_tensor("w3s", [1024], F32, kind="ExternalInput").ap()
    dtc = nc.dram_tensor("tc", [128, T], BF16, kind="ExternalInput").ap()
    dts = nc.dram_tensor("ts", [128, T], BF16, kind="ExternalInput").ap()
    dwqs = nc.dram_tensor("wqsb", [8, 128, 8, 128], F32,
                          kind="ExternalInput").ap()
    dwks = nc.dram_tensor("wksb", [8, 128, 8, 128], F32,
                          kind="ExternalInput").ap()
    dcm = nc.dram_tensor("cm01", [128, 896], BF16, kind="ExternalInput").ap()
    dmv = nc.dram_tensor("mval", [128, 8, 32], F32, kind="ExternalInput").ap()
    did = nc.dram_tensor("ident", [128, 128], BF16, kind="ExternalInput").ap()

    dout = nc.dram_tensor("out", [NQ, DM], F32, kind="ExternalOutput").ap()
    s_kt = nc.dram_tensor("kt_s", [H, 128, T], BF16, kind=SCR).ap()
    s_v = nc.dram_tensor("v_s", [H, T, 128], BF16, kind=SCR).ap()
    s_xp = nc.dram_tensor("xp_s", [NQ, DM], F32, kind=SCR).ap()
    s_x3 = nc.dram_tensor("x3_s", [8, 128, NQ], BF16, kind=SCR).ap()
    dqo = (nc.dram_tensor("qo_d", [128, 8, 4096], BF16,
                          kind="ExternalOutput").ap() if dbg else None)
    dx1 = (nc.dram_tensor("x1t_d", [128, 8, 512], BF16,
                          kind="ExternalOutput").ap() if dbg else None)
    s_w1 = nc.dram_tensor("w1_s", [32, 128, 8, 128], BF16, kind="Internal").ap()
    s_w2 = nc.dram_tensor("w2_s", [32, 128, 8, 128], BF16, kind="Internal").ap()
    s_h1 = nc.dram_tensor("h1_s", [32, 128, 32, 128], BF16, kind="Internal").ap()
    s_h2 = nc.dram_tensor("h2_s", [32, 128, 32, 128], BF16, kind="Internal").ap()
    s_w3 = nc.dram_tensor("w3_s", [32, 128, 1024], BF16, kind="Internal").ap()

    with ExitStack() as ctx:
        ecx = ctx.enter_context

        SB_BASE = 16512
        SB_LIMIT = 212832

        def at(name, off, shape, dtype):
            end = off + int(np.prod(shape[1:])) * mybir.dt.size(dtype)
            assert end <= SB_LIMIT, (name, end)
            return nc.alloc_sbuf_tensor_at(name, shape, dtype,
                                           offset=SB_BASE + off)

        ident = at("identt", 0, [128, 128], BF16)
        ones_c = at("ones_c", 256, [128, 1], BF16)
        ones_r = at("ones_r", 288, [1, 128], BF16)
        eps_t = at("eps_t", 544, [128, 1], F32)
        zero_t = at("zero_t", 576, [128, 1], F32)
        mval = at("mval_sb", 608, [128, 8, 32], F32)
        scT = at("scT_sb", 1632, [128, 128], F32)
        w3sb = at("w3sb", 2144, [128, 1024], F32)
        cm01 = at("cm01_sb", 6240, [128, 896], BF16)
        stt = []
        off = 8032
        for par in range(2):
            d = {}
            d["st6"] = at(f"st6_{par}", off, [128, 12], F32); off += 64
            d["ab"] = at(f"ab_{par}", off, [128, 4], F32); off += 32
            for nm in ("dm", "var", "m", "negm", "rstd", "bln"):
                d[nm] = at(f"{nm}_{par}", off, [128, 1], F32); off += 32
            stt.append(d)
        rrow = [at(f"rrow{p}", off + 1024 * p, [1, 512], BF16) for p in range(2)]
        off += 2048
        spc_a = at("spc_a", off, [128, 128], F32); off += 512
        spc_b = at("spc_b", off, [128, 128], F32); off += 512
        assert off <= 16384

        AR = 16384
        wqb = at("wqb_sb", AR, [128, 8, 8, 128], BF16)
        wkb = at("wkb_sb", AR + 16384, [128, 8, 8, 128], BF16)
        wqs = at("wqs_sb", AR + 32768, [128, 8, 8, 128], BF16)
        wks = at("wks_sb", AR + 49152, [128, 8, 8, 128], BF16)
        wvb = at("wvb_sb", AR + 65536, [128, 8, 1024], BF16)
        xb = [at(f"xb{p}", AR + 81920 + 8192 * p, [128, 4, 1024], BF16)
              for p in range(2)]
        x1s = [at(f"x1s{p}", AR + 98304 + 2048 * p, [128, 1024], BF16)
               for p in range(2)]
        x1t = at("x1t", AR + 102400, [128, 8, 512], BF16)
        krope = at("krope", AR + 110592, [128, 4, 512], BF16)
        vstage = at("vstage", AR + 114688, [128, 4, 1024], BF16)
        rp1 = at("rp1", AR + 122880, [128, 512], BF16)
        rp2 = at("rp2", AR + 123904, [128, 512], BF16)
        tcb = [at(f"tcb{p}", AR + 124928 + 1024 * p, [128, 512], BF16)
               for p in range(2)]
        tsb = [at(f"tsb{p}", AR + 126976 + 1024 * p, [128, 512], BF16)
               for p in range(2)]
        kbuf = [at(f"kbuf{p}", AR + 16384 * p, [128, T], BF16) for p in range(2)]
        vbuf = [at(f"vbuf{p}", AR + 32768 + 16384 * p, [128, 64, 128], BF16)
                for p in range(2)]
        pT = [at(f"pT{p}", AR + 65536 + 2048 * p, [128, 1024], BF16)
              for p in range(2)]
        bcs = at("bcs", AR + 69632, [128, 512], F32)
        P3 = AR + 73728
        wob = at("wob_sb", P3, [128, 8, 1024], BF16)
        xo = [at(f"xo{p}", P3 + 16384 + 4096 * p, [128, 1024], F32)
              for p in range(2)]
        xps = [at(f"xps{p}", P3 + 24576 + 4096 * p, [128, 1024], F32)
               for p in range(2)]
        x3s = [at(f"x3s{p}", P3 + 32768 + 2048 * p, [128, 1024], BF16)
               for p in range(2)]
        x3t = [at(f"x3t{p}", P3 + 36864 + 2048 * p, [128, 8, 128], BF16)
               for p in range(2)]
        QO = at("QO", SB_LIMIT - 65536, [128, 8, 4096], BF16)
        ha = at("ha", AR, [128, 32, 1024], BF16)
        hb = at("hb", AR + 65536, [128, 32, 1024], BF16)
        x3tt = at("x3tt", AR + 131072, [128, 8, 1024], BF16)
        ws = [at(f"ws{p}", AR + 147456 + 8192 * p, [128, 4096], BF16)
              for p in range(2)]
        xpb = [at(f"xpb{p}", AR + 163840 + 4096 * p, [128, 1024], F32)
               for p in range(2)]
        outb = [at(f"outb{p}", AR + 172032 + 4096 * p, [128, 1024], F32)
                for p in range(2)]
        f4 = [at(f"f4{p}", AR + 180224 + 4096 * p, [128, 1024], F32)
              for p in range(2)]

        pA = ecx(nc.psum_tensor("pA", [128, 1024], F32))
        pB = ecx(nc.psum_tensor("pB", [128, 1024], F32))
        pC = ecx(nc.psum_tensor("pC", [128, 1024], F32))
        pD = ecx(nc.psum_tensor("pD", [128, 1024], F32))
        spair = [pA, pB]

        sem_names = [
            "c_ld", "prep", "x_ld", "sst", "apl", "tps", "tev", "pk", "rk",
            "pv2", "vev", "spk", "spv", "kvl", "smm", "msk", "expd", "spd",
            "rcp", "bce", "nrm", "xol", "pr3", "sv3", "t3d", "xpd", "spxp",
            "sv3b", "ap3", "tp3", "te3", "spx3", "wl4", "x3l", "xpl", "pe4",
            "rsd1", "rsd3", "rsd4", "tbl0", "tbl1", "xld0", "xld1", "kvl0", "kvl1",
            "spk0", "spk1", "spk2", "spk3", "spxp0", "spxp1", "spx30",
            "spx31", "xol0", "xol1", "xpl0", "xpl1", "outd0", "outd1",
            "wl40", "wl41",
            "ev4", "f4d", "s4d", "o4d", "outd", "ini",
        ]
        SM = {n: ecx(nc.semaphore(n)) for n in sem_names}
        N_CLD = 11 * 16

        blk = ecx(nc.Block())

        def vsp(ve):
            return ve.tensor_copy(out=spc_b[:], in_=spc_a[:])

        def ssp(sc):
            return sc.activation(out=spc_b[:, 0:64], in_=spc_a[:, 0:64],
                                 func=AF.Copy)

        def stats_dve(ve, d, src_a, src_b, sem):
            ve.bn_stats(out=d["st6"][:, 0:6], in_=src_a)
            ve.bn_stats(out=d["st6"][:, 6:12], in_=src_b)
            vsp(ve)
            ve.bn_aggr(out=d["ab"][:, 0:2], in_=d["st6"][:, 0:12])
            vsp(ve)
            vsp(ve).then_inc(sem, 1)

        def rstd_scalar(sc, d):
            sc.activation(out=d["dm"][:], in_=d["ab"][:, 1:2], func=AF.Ln,
                          bias=eps_t[:, 0:1])
            ssp(sc)
            sc.activation(out=d["rstd"][:], in_=d["dm"][:], func=AF.Exp,
                          scale=-0.5)
            ssp(sc)
            return ssp(sc)

        def ln_apply_dve(ve, d, out_ap, in_ap):
            return ve.tensor_scalar(
                out=out_ap, in0=in_ap,
                scalar1=d["ab"][:, 0:1], scalar2=d["rstd"][:, 0:1],
                op0=ALU.subtract, op1=ALU.mult)

        def p3_tp(te, qc):
            te.wait_ge(SM["ap3"], qc + 1)
            if qc >= 2:
                te.wait_ge(SM["te3"], qc - 1)
            dst = pD.bitcast(BF16)
            half = qc % 2
            for fc in range(8):
                mm = te.transpose(
                    out=dst[:, half * 1024 + fc * 128:half * 1024 + (fc + 1) * 128],
                    in_=x3s[qc % 2][:, fc * 128:(fc + 1) * 128],
                    identity=ident[:])
            mm.then_inc(SM["tp3"], 1)

        def p3_te3(ve, qc):
            ve.wait_ge(SM["tp3"], qc + 1)
            if qc >= 2:
                ve.wait_ge(SM["spx30" if qc % 2 == 0 else "spx31"],
                           16 * (qc // 2))
            half = qc % 2
            ve.tensor_copy(
                out=x3t[qc % 2][:],
                in_=pD.bitcast(BF16)[:, half * 1024:(half + 1) * 1024]
                .rearrange("p (c d) -> p c d", c=8)).then_inc(SM["te3"], 1)

        @blk.gpsimd
        def _(gp):
            gp.dma_start(out=ident[:], in_=did).then_inc(SM["c_ld"], 16)
            gp.dma_start(out=cm01[:], in_=dcm).then_inc(SM["c_ld"], 16)
            gp.dma_start(out=mval[:], in_=dmv).then_inc(SM["c_ld"], 16)
            gp.dma_start(out=scT[:], in_=dsct).then_inc(SM["c_ld"], 16)
            gp.dma_start(out=w3sb[:],
                         in_=dw3s[None, :].partition_broadcast(128)
                         ).then_inc(SM["c_ld"], 16)
            gp.dma_start(out=wqb[:], in_=dwq.rearrange("o p i d -> p o i d")).then_inc(SM["c_ld"], 16)
            gp.dma_start(out=wkb[:], in_=dwk.rearrange("o p i d -> p o i d")).then_inc(SM["c_ld"], 16)
            gp.dma_start(out=wqs[:], in_=dwqs.rearrange("o p i d -> p o i d")).then_inc(SM["c_ld"], 16)
            gp.dma_start(out=wks[:], in_=dwks.rearrange("o p i d -> p o i d")).then_inc(SM["c_ld"], 16)
            gp.dma_start(out=wvb[:], in_=dwv.rearrange("i p m -> p i m")).then_inc(SM["c_ld"], 16)
            def load_x(t):
                gp.dma_start(
                    out=xb[t % 2][:],
                    in_=dx[t * 512:(t + 1) * 512, :].rearrange(
                        "(i p) m -> p i m", p=128),
                ).then_inc(SM["xld0" if t % 2 == 0 else "xld1"], 16)

            load_x(0)
            load_x(1)
            for t in range(NT):
                t0 = t * 512
                if t + 2 < NT:
                    gp.wait_ge(SM["apl"], 4 * (t + 1))
                    load_x(t + 2)
                for h in range(H):
                    gp.wait_ge(SM["rk"], P1_GCUM[t] + h + 1)
                    gk = t * 8 + h
                    gp.dma_start(out=s_kt[h, :, t0:t0 + 512],
                                 in_=krope[:, gk % 4, :]
                                 ).then_inc(SM["spk" + str(gk % 4)], 16)
                gp.wait_ge(SM["vev"], 8 * (t + 1))
                for h in range(H):
                    gp.dma_start(
                        out=s_v[h, t0:t0 + 512, :].rearrange(
                            "(i p) d -> p i d", p=128),
                        in_=vstage[:, :, h * 128:(h + 1) * 128],
                    ).then_inc(SM["spv"], 16)
                if dbg and t == 0:
                    gp.dma_start(out=dx1, in_=x1t[:]).then_inc(SM["s4d"], 16)
            gp.wait_ge(SM["apl"], 64)
            gp.wait_ge(SM["vev"], 8 * NT)
            gp.wait_ge(SM["rk"], P1_GCUM[NT])
            gp.dma_start(out=wob[:], in_=dwo).then_inc(SM["c_ld"], 16)
            gp.dma_start(out=s_w1, in_=dw1).then_inc(SM["prep"], 16)
            gp.dma_start(out=s_w2, in_=dw2).then_inc(SM["prep"], 16)
            gp.dma_start(out=s_h1, in_=dh1).then_inc(SM["prep"], 16)
            gp.dma_start(out=s_h2, in_=dh2).then_inc(SM["prep"], 16)
            gp.dma_start(out=s_w3, in_=dw3).then_inc(SM["prep"], 16)
            if dbg:
                gp.wait_ge(SM["nrm"], NSLOT)
                gp.dma_start(out=dqo, in_=QO[:]).then_inc(SM["s4d"], 16)

        @blk.sync
        def _(sy):
            for t in range(NT):
                if t >= 2:
                    sy.wait_ge(SM["rk"], P1_GCUM[t - 1])
                tb = "tbl0" if t % 2 == 0 else "tbl1"
                sy.dma_start(out=tcb[t % 2][:],
                             in_=dtc[:, t * 512:(t + 1) * 512]
                             ).then_inc(SM[tb], 16)
                sy.dma_start(out=tsb[t % 2][:],
                             in_=dts[:, t * 512:(t + 1) * 512]
                             ).then_inc(SM[tb], 16)
            for _r in range(4):
                sy.wait_ge(SM["spk" + str(_r)], 16 * 32)
            sy.wait_ge(SM["spv"], 16 * NT * H)
            for h in range(H):
                if h >= 2:
                    sy.wait_ge(SM["nrm"], 8 * (h - 1))
                kb, vb = kbuf[h % 2], vbuf[h % 2]
                kvs = "kvl0" if h % 2 == 0 else "kvl1"
                for i in range(8):
                    a, b = 4 * i * 128, 4 * (i + 1) * 128
                    sy.dma_start(out=kb[:, a:b],
                                 in_=s_kt[h, :, a:b]).then_inc(SM[kvs], 16)
                    sy.dma_start(
                        out=vb[:, 4 * i:4 * (i + 1), :],
                        in_=s_v[h, a:b, :].rearrange("(c p) d -> p c d", p=128),
                    ).then_inc(SM[kvs], 16)
                    if C2NEW[i]:
                        oc0 = 32 + (0 if i == 0 else 16)
                        ga, gb = oc0 * 128, (oc0 + 16) * 128
                        sy.dma_start(out=kb[:, ga:gb],
                                     in_=s_kt[h, :, ga:gb]
                                     ).then_inc(SM[kvs], 16)
                        for half in range(2):
                            c0 = oc0 + 8 * half
                            sy.dma_start(
                                out=vb[:, c0:c0 + 8, :],
                                in_=s_v[h, c0 * 128:(c0 + 8) * 128, :]
                                .rearrange("(c p) d -> p c d", p=128),
                            ).then_inc(SM[kvs], 16)
            for qc in range(32):
                if qc >= 2:
                    sy.wait_ge(SM["xpd"], qc - 1)
                sy.dma_start(out=xo[qc % 2][:],
                             in_=dx[qc * 128:(qc + 1) * 128, :]
                             ).then_inc(SM["xol0" if qc % 2 == 0 else "xol1"],
                                        16)
            sy.wait_ge(SM["nrm"], NSLOT)
            sy.wait_ge(SM["spx30"], 16 * 16)
            sy.wait_ge(SM["spx31"], 16 * 16)
            sy.wait_ge(SM["spxp0"], 16 * 16)
            sy.wait_ge(SM["spxp1"], 16 * 16)
            wl = 0
            scr = {"w1": s_w1, "w2": s_w2, "h1": s_h1, "h2": s_h2}
            prep_need = {"w1": 1, "w2": 2, "h1": 3, "h2": 4}
            for t4 in range(4):
                if t4 >= 1:
                    sy.wait_ge(SM["pe4"], GR_TILE * (t4 - 1) + 64)
                sy.dma_start(
                    out=x3tt[:],
                    in_=s_x3[:, :, t4 * 1024:(t4 + 1) * 1024].rearrange(
                        "c p t -> p c t"),
                ).then_inc(SM["x3l"], 16)
                for (lname, n_oc, n_ic) in P4_LAYERS:
                    sy.wait_ge(SM["prep"], 16 * 5)
                    for oc in range(n_oc):
                        if wl >= 2:
                            sy.wait_ge(SM["pe4"], wl - 1)
                        sy.dma_start(
                            out=ws[wl % 2][:, 0:n_ic * 128],
                            in_=scr[lname][oc].rearrange("p i d -> p (i d)"),
                        ).then_inc(SM["wl40" if wl % 2 == 0 else "wl41"], 16)
                        wl += 1
                sy.wait_ge(SM["prep"], 16 * 5)
                for pr in range(4):
                    for ic in range(32):
                        if wl >= 2:
                            sy.wait_ge(SM["pe4"], wl - 1)
                        sy.dma_start(out=ws[wl % 2][:, 0:1024],
                                     in_=s_w3[ic]
                                     ).then_inc(
                            SM["wl40" if wl % 2 == 0 else "wl41"], 16)
                        wl += 1
                    for k in range(2):
                        qg = t4 * 8 + pr * 2 + k
                        if qg >= 2:
                            sy.wait_ge(SM["o4d"], qg - 1)
                        sy.dma_start(out=xpb[(pr * 2 + k) % 2][:],
                                     in_=s_xp[qg * 128:(qg + 1) * 128, :]
                                     ).then_inc(
                            SM["xpl0" if k == 0 else "xpl1"], 16)

        @blk.tensor
        def _(te):
            te.wait_ge(SM["c_ld"], N_CLD - 16)
            te.wait_ge(SM["ini"], 1)
            pkc = 0
            for t in range(NT):
                for i in range(4):
                    te.wait_ge(SM["apl"], 4 * t + i + 1)
                    if 4 * t + i >= 2:
                        te.wait_ge(SM["tev"], 4 * t + i - 1)
                    dst = pD.bitcast(BF16)
                    half = (4 * t + i) % 2
                    for fc in range(8):
                        mm = te.transpose(
                            out=dst[:, half * 1024 + fc * 128:
                                    half * 1024 + (fc + 1) * 128],
                            in_=x1s[i % 2][:, fc * 128:(fc + 1) * 128],
                            identity=ident[:])
                    mm.then_inc(SM["tps"], 1)
                te.wait_ge(SM["tev"], 4 * (t + 1))
                for (kind, h) in p1_gseq(t):
                    wb = wkb if kind == "K" else wqb
                    wbs = wks if kind == "K" else wqs
                    bank = pA if pkc % 2 == 0 else pB
                    if pkc >= 2:
                        te.wait_ge(SM["rk"], pkc - 1)
                    for ic in range(8):
                        te.matmul(bank[:, 0:512], wb[:, h, ic, :],
                                  x1t[:, ic, :],
                                  start=(ic == 0), stop=(ic == 7))
                        mm = te.matmul(bank[:, 512:1024], wbs[:, h, ic, :],
                                       x1t[:, ic, :],
                                       start=(ic == 0), stop=(ic == 7))
                    mm.then_inc(SM["pk"], 1)
                    pkc += 1
                for i2 in range(4):
                    for nh in range(2):
                        v = 8 * t + 2 * i2 + nh
                        if v >= 2:
                            te.wait_ge(SM["vev"], v - 1)
                        for ic in range(8):
                            mm = te.matmul(
                                pC[:, nh * 512:(nh + 1) * 512],
                                x1t[:, ic, i2 * 128:(i2 + 1) * 128],
                                wvb[:, ic, nh * 512:(nh + 1) * 512],
                                start=(ic == 0), stop=(ic == 7))
                        mm.then_inc(SM["pv2"], 1)
            def emit_S(d):
                if d["first"]:
                    te.wait_ge(SM["kvl0" if d["h"] % 2 == 0 else "kvl1"],
                               16 * 22 * (d["h"] // 2 + 1))
                gp_ = d["gp"]
                if gp_ >= 2:
                    te.wait_ge(SM["expd"], gp_ - 1)
                sp = spair[gp_ % 2]
                kb = kbuf[d["h"] % 2]
                qslice = QO[:, d["h"], d["i"] * 512:(d["i"] + 1) * 512]
                for k in range(2):
                    c = d["c0"] + k
                    mm = te.matmul(sp[:, k * 512:(k + 1) * 512],
                                   kb[:, c * 128:(c + 1) * 128], qslice,
                                   start=True, stop=True)
                mm.then_inc(SM["smm"], 1)

            def emit_SP(d):
                gp_ = d["gp"]
                sm = P2_SLOTS[(d["h"], d["i"])]
                if d["masked"]:
                    te.wait_ge(SM["msk"], d["msk_after"])
                else:
                    te.wait_ge(SM["expd"], gp_ + 1)
                if d["first"] and sm["n_slot"] >= 2:
                    te.wait_ge(SM["rcp"], sm["n_slot"] - 1)
                    te.wait_ge(SM["nrm"], sm["n_slot"] - 1)
                vb = vbuf[d["h"] % 2]
                pt = pT[gp_ % 2]
                for k in range(2):
                    c = d["c0"] + k
                    st = d["first"] and k == 0
                    sp_ = d["last_of_slot"] and k == 1
                    te.matmul(pC[0:1, 0:512], ones_c[:],
                              pt[:, k * 512:(k + 1) * 512], start=st, stop=sp_)
                    mm = te.matmul(pD[:, 0:512], vb[:, c, :],
                                   pt[:, k * 512:(k + 1) * 512],
                                   start=st, stop=sp_)
                mm.then_inc(SM["spd"], 1)

            def emit_bcast(d):
                sm = P2_SLOTS[(d["h"], d["i"])]
                te.wait_ge(SM["rcp"], sm["n_slot"])
                te.matmul(pC[:, 512:1024], ones_r[:],
                          rrow[(sm["n_slot"] - 1) % 2][:],
                          start=True, stop=True).then_inc(SM["bce"], 1)

            prev = None
            pending_bcast = None
            for d in P2_PAIRS:
                if prev is not None and prev["last_of_slot"]:
                    emit_SP(prev)
                    pending_bcast = prev
                    prev = None
                emit_S(d)
                if pending_bcast is not None:
                    emit_bcast(pending_bcast)
                    pending_bcast = None
                if prev is not None:
                    emit_SP(prev)
                prev = d
            emit_SP(prev)
            emit_bcast(prev)
            te.wait_ge(SM["nrm"], NSLOT)
            te.wait_ge(SM["c_ld"], N_CLD)
            for qc in range(32):
                dst = pA if qc % 2 == 0 else pB
                if qc >= 2:
                    te.wait_ge(SM["t3d"], qc - 1)
                for h in range(H):
                    for nh in range(2):
                        mm = te.matmul(dst[:, nh * 512:(nh + 1) * 512],
                                       QO[:, h, qc * 128:(qc + 1) * 128],
                                       wob[:, h, nh * 512:(nh + 1) * 512],
                                       start=(h == 0), stop=(h == 7))
                mm.then_inc(SM["pr3"], 1)
                if qc >= 1:
                    p3_tp(te, qc - 1)
            p3_tp(te, 31)
            pe4c = 0
            ev4_base = 0
            for t4 in range(4):
                te.wait_ge(SM["x3l"], 16 * (t4 + 1))
                if t4 >= 1:
                    te.wait_ge(SM["f4d"], 8 * t4)
                ev4_seen = 0
                for (lname, n_oc, n_ic) in P4_LAYERS:
                    if lname == "h1":
                        te.wait_ge(SM["ev4"], ev4_base + 64)
                    if lname == "h2":
                        te.wait_ge(SM["ev4"], ev4_base + 96)
                    src = x3tt if lname in ("w1", "w2") else (
                        hb if lname == "h1" else ha)
                    for oc in range(n_oc):
                        te.wait_ge(SM["wl40" if pe4c % 2 == 0 else "wl41"],
                                   16 * (pe4c // 2 + 1))
                        dstp = pA if oc % 2 == 0 else pB
                        if ev4_seen >= 2:
                            te.wait_ge(SM["ev4"], ev4_base + ev4_seen - 1)
                        for ic in range(n_ic):
                            lhs = ws[pe4c % 2][:, ic * 128:(ic + 1) * 128]
                            te.matmul(dstp[:, 0:512], lhs, src[:, ic, 0:512],
                                      start=(ic == 0), stop=(ic == n_ic - 1))
                            mm = te.matmul(dstp[:, 512:1024], lhs,
                                           src[:, ic, 512:1024],
                                           start=(ic == 0),
                                           stop=(ic == n_ic - 1))
                        mm.then_inc(SM["pe4"], 1)
                        pe4c += 1
                        ev4_seen += 1
                te.wait_ge(SM["ev4"], ev4_base + 128)
                for pr in range(4):
                    pe_pair = [pA, pB] if pr % 2 == 0 else [pC, pD]
                    if pr >= 2:
                        te.wait_ge(SM["f4d"], 8 * t4 + (pr - 2) * 2 + 2)
                    for ic in range(32):
                        te.wait_ge(SM["wl40" if pe4c % 2 == 0 else "wl41"],
                                   16 * (pe4c // 2 + 1))
                        for k in range(2):
                            i3 = pr * 2 + k
                            for nh in range(2):
                                mm = te.matmul(
                                    pe_pair[k][:, nh * 512:(nh + 1) * 512],
                                    hb[:, ic, i3 * 128:(i3 + 1) * 128],
                                    ws[pe4c % 2][:, nh * 512:(nh + 1) * 512],
                                    start=(ic == 0), stop=(ic == 31))
                        mm.then_inc(SM["pe4"], 1)
                        pe4c += 1
                ev4_base += 128

        @blk.vector
        def _(ve):
            ve.memset(ones_c[:], 1.0)
            ve.memset(ones_r[:], 1.0)
            ve.memset(eps_t[:], EPS)
            ve.memset(zero_t[:], 0.0)
            ve.memset(spc_a[:], 0.0)
            ve.memset(spc_b[:], 0.0)
            vsp(ve)
            vsp(ve).then_inc(SM["ini"], 1)
            ve.wait_ge(SM["c_ld"], N_CLD - 16)
            def p1_stats(ti):
                t, i = ti // 4, ti % 4
                d = stt[i % 2]
                ve.wait_ge(SM["xld0" if t % 2 == 0 else "xld1"],
                           16 * (t // 2 + 1))
                if ti >= 2:
                    ve.wait_ge(SM["apl"], ti - 1)
                src = xb[t % 2][:, i, :]
                stats_dve(ve, d, src[0:128, 0:512], src[0:128, 512:1024],
                          SM["sst"])

            p1_stats(0)
            for t in range(NT):
                for i in range(4):
                    d = stt[i % 2]
                    if 4 * t + i + 1 < 64:
                        p1_stats(4 * t + i + 1)
                    src = xb[t % 2][:, i, :]
                    ve.wait_ge(SM["rsd1"], 4 * t + i + 1)
                    if 4 * t + i >= 2:
                        ve.wait_ge(SM["tps"], 4 * t + i - 1)
                    ln_apply_dve(ve, d, x1s[i % 2][:], src
                                 ).then_inc(SM["apl"], 1)
                    if i >= 1:
                        ti = 4 * t + i - 1
                        ve.wait_ge(SM["tps"], ti + 1)
                        half = ti % 2
                        ve.tensor_copy(
                            out=x1t[:, :, (ti % 4) * 128:(ti % 4) * 128 + 128],
                            in_=pD.bitcast(BF16)[:, half * 1024:
                                                 (half + 1) * 1024]
                            .rearrange("p (c d) -> p c d", c=8)
                        ).then_inc(SM["tev"], 1)
                ti = 4 * t + 3
                ve.wait_ge(SM["tps"], ti + 1)
                half = ti % 2
                ve.tensor_copy(
                    out=x1t[:, :, 384:512],
                    in_=pD.bitcast(BF16)[:, half * 1024:(half + 1) * 1024]
                    .rearrange("p (c d) -> p c d", c=8)).then_inc(SM["tev"], 1)
                ve.wait_ge(SM["tbl0" if t % 2 == 0 else "tbl1"],
                           32 * (t // 2 + 1))
                for gi, (kind, h) in enumerate(p1_gseq(t)):
                    g = P1_GCUM[t] + gi
                    ve.wait_ge(SM["pk"], g + 1)
                    if kind == "K":
                        gk = t * 8 + h
                        if gk >= 4:
                            ve.wait_ge(SM["spk" + str(gk % 4)],
                                       16 * (gk // 4))
                    bank = pA if g % 2 == 0 else pB
                    tt0 = t * 512
                    ve.tensor_tensor(out=rp1[:], in0=bank[:, 0:512],
                                     in1=tcb[t % 2][:], op=ALU.mult)
                    ve.tensor_tensor(out=rp2[:], in0=bank[:, 512:1024],
                                     in1=tsb[t % 2][:], op=ALU.mult)
                    if kind == "K":
                        dst = krope[:, (t * 8 + h) % 4, :]
                    else:
                        dst = QO[:, h, tt0:tt0 + 512]
                    ve.tensor_tensor(out=dst, in0=rp1[:], in1=rp2[:],
                                     op=ALU.add).then_inc(SM["rk"], 1)
            for d in P2_PAIRS:
                gp_ = d["gp"]
                if d["masked"]:
                    ve.wait_ge(SM["expd"], gp_ + 1)
                    pt = pT[gp_ % 2]
                    n_own = 4 * (d["i"] + 1)
                    for k in range(2):
                        c = d["c0"] + k
                        r = c - (n_own - 4)
                        s = 384 - 128 * r
                        mm = ve.tensor_tensor(
                            out=pt[:, k * 512:(k + 1) * 512],
                            in0=pt[:, k * 512:(k + 1) * 512],
                            in1=cm01[:, s:s + 512], op=ALU.mult)
                    mm.then_inc(SM["msk"], 1)
                if d["last_of_slot"]:
                    sm = P2_SLOTS[(d["h"], d["i"])]
                    ve.wait_ge(SM["spd"], sm["last_gp"] + 1)
                    with nc.allow_low_precision(
                            reason="softmax 1/sum as bf16 matmul operand"):
                        ve.reciprocal(out=rrow[(sm["n_slot"] - 1) % 2][:],
                                      in_=pC[0:1, 0:512])
                    vsp(ve).then_inc(SM["rcp"], 1)
                    ve.wait_ge(SM["bce"], sm["n_slot"])
                    ve.tensor_copy(out=bcs[:], in_=pC[:, 512:1024])
                    ve.tensor_tensor(
                        out=QO[:, d["h"], d["i"] * 512:(d["i"] + 1) * 512],
                        in0=pD[:, 0:512], in1=bcs[:], op=ALU.mult
                    ).then_inc(SM["nrm"], 1)
            for qc in range(32):
                par = qc % 2
                d = stt[par]
                srcp = pA if par == 0 else pB
                ve.wait_ge(SM["pr3"], qc + 1)
                stats_dve(ve, d, srcp[:, 0:512], srcp[:, 512:1024], SM["sv3"])
                ve.wait_ge(SM["rsd3"], 2 * qc + 1)
                if qc >= 2:
                    ve.wait_ge(SM["spxp0" if qc % 2 == 0 else "spxp1"],
                               16 * (qc // 2))
                ln_apply_dve(ve, d, xps[par][:], srcp[:]
                             ).then_inc(SM["t3d"], 1)
                ve.wait_ge(SM["xol0" if qc % 2 == 0 else "xol1"],
                           16 * (qc // 2 + 1))
                ve.tensor_add(xps[par][:], xps[par][:], xo[par][:]
                              ).then_inc(SM["xpd"], 1)
                stats_dve(ve, d, xps[par][:, 0:512], xps[par][:, 512:1024],
                          SM["sv3b"])
                ve.wait_ge(SM["rsd3"], 2 * qc + 2)
                if qc >= 2:
                    ve.wait_ge(SM["tp3"], qc - 1)
                ln_apply_dve(ve, d, x3s[par][:], xps[par][:]
                             ).then_inc(SM["ap3"], 1)
                if qc >= 1:
                    p3_te3(ve, qc - 1)
                if qc == 31:
                    p3_te3(ve, 31)
            pe4c = 0
            for t4 in range(4):
                for (lname, n_oc, n_ic) in P4_LAYERS:
                    for oc in range(n_oc):
                        if lname == "w2":
                            ve.wait_ge(SM["pe4"], pe4c + 1)
                            srcp = pA if oc % 2 == 0 else pB
                            ve.scalar_tensor_tensor(
                                out=hb[:, oc, :], in0=srcp[:],
                                scalar=scT[:, 32 + oc:33 + oc],
                                in1=ha[:, oc, :], op0=ALU.mult, op1=ALU.mult
                            ).then_inc(SM["ev4"], 1)
                        pe4c += 1
                pe4c += 128
                for i3 in range(8):
                    qg = t4 * 8 + i3
                    par = i3 % 2
                    d = stt[par]
                    pr, k = i3 // 2, i3 % 2
                    srcp = ([pA, pB] if pr % 2 == 0 else [pC, pD])[k]
                    ve.wait_ge(SM["pe4"], GR_TILE * t4 + 128 + 32 * (pr + 1))
                    ve.tensor_tensor(out=f4[par][:], in0=srcp[:], in1=w3sb[:],
                                     op=ALU.mult).then_inc(SM["f4d"], 1)
                    stats_dve(ve, d, f4[par][:, 0:512], f4[par][:, 512:1024],
                              SM["sv3"])
                    ve.wait_ge(SM["rsd4"], qg + 1)
                    if qg >= 2:
                        ve.wait_ge(SM["outd0" if qg % 2 == 0 else "outd1"],
                                   16 * (qg // 2))
                    ln_apply_dve(ve, d, outb[par][:], f4[par][:])
                    ve.wait_ge(SM["xpl0" if i3 % 2 == 0 else "xpl1"],
                               16 * (qg // 2 + 1))
                    ve.tensor_add(outb[par][:], outb[par][:], xpb[par][:]
                                  ).then_inc(SM["o4d"], 1)

        @blk.scalar
        def _(sc):
            sc.wait_ge(SM["ini"], 1)
            for t in range(NT):
                for i in range(4):
                    d = stt[i % 2]
                    sc.wait_ge(SM["sst"], 4 * t + i + 1)
                    rstd_scalar(sc, d).then_inc(SM["rsd1"], 1)
                for i2 in range(4):
                    for nh in range(2):
                        v = 8 * t + 2 * i2 + nh
                        sc.wait_ge(SM["pv2"], v + 1)
                        if t >= 1 and i2 == 0 and nh == 0:
                            sc.wait_ge(SM["spv"], 16 * 8 * t)
                        sc.activation(
                            out=vstage[:, i2, nh * 512:(nh + 1) * 512],
                            in_=pC[:, nh * 512:(nh + 1) * 512],
                            func=AF.Copy).then_inc(SM["vev"], 1)
            for d in P2_PAIRS:
                gp_ = d["gp"]
                sc.wait_ge(SM["smm"], gp_ + 1)
                if gp_ >= 2:
                    sc.wait_ge(SM["spd"], gp_ - 1)
                bias = mval[:, d["i"], d["pidx"]:d["pidx"] + 1]
                sc.activation(out=pT[gp_ % 2][:], in_=spair[gp_ % 2][:],
                              func=AF.Exp, bias=bias).then_inc(SM["expd"], 1)
            for qc in range(32):
                par = qc % 2
                d = stt[par]
                sc.wait_ge(SM["sv3"], qc + 1)
                rstd_scalar(sc, d).then_inc(SM["rsd3"], 1)
                sc.wait_ge(SM["sv3b"], qc + 1)
                rstd_scalar(sc, d).then_inc(SM["rsd3"], 1)
                sc.wait_ge(SM["xpd"], qc + 1)
                sc.dma_start(out=s_xp[qc * 128:(qc + 1) * 128, :],
                             in_=xps[par][:]
                             ).then_inc(SM["spxp0" if par == 0 else "spxp1"],
                                        16)
                if qc >= 1:
                    sc.wait_ge(SM["te3"], qc)
                    sc.dma_start(out=s_x3[:, :, (qc - 1) * 128:qc * 128]
                                 .rearrange("c p d -> p c d"),
                                 in_=x3t[(qc - 1) % 2][:]
                                 ).then_inc(
                        SM["spx30" if (qc - 1) % 2 == 0 else "spx31"], 16)
            sc.wait_ge(SM["te3"], 32)
            sc.dma_start(out=s_x3[:, :, 31 * 128:32 * 128]
                         .rearrange("c p d -> p c d"),
                         in_=x3t[31 % 2][:]).then_inc(SM["spx31"], 16)
            pe4c = 0
            for t4 in range(4):
                for (lname, n_oc, n_ic) in P4_LAYERS:
                    for oc in range(n_oc):
                        if lname != "w2":
                            sc.wait_ge(SM["pe4"], pe4c + 1)
                            srcp = pA if oc % 2 == 0 else pB
                            dst = ha[:, oc, :] if lname in ("w1", "h1") \
                                else hb[:, oc, :]
                            fn = AF.Silu if lname == "w1" else AF.Gelu
                            col = {"w1": 0, "h1": 64, "h2": 96}[lname] + oc
                            sc.activation(out=dst, in_=srcp[:], func=fn,
                                          scale=scT[:, col:col + 1]
                                          ).then_inc(SM["ev4"], 1)
                        pe4c += 1
                pe4c += 128
                for i3 in range(8):
                    qg = t4 * 8 + i3
                    par = i3 % 2
                    d = stt[par]
                    sc.wait_ge(SM["sv3"], 32 + qg + 1)
                    rstd_scalar(sc, d).then_inc(SM["rsd4"], 1)
                    sc.wait_ge(SM["o4d"], qg + 1)
                    sc.dma_start(out=dout[qg * 128:(qg + 1) * 128, :],
                                 in_=outb[par][:]
                                 ).then_inc(
                        SM["outd0" if par == 0 else "outd1"], 16)
            sc.wait_ge(SM["outd0"], 16 * 16)
            sc.wait_ge(SM["outd1"], 16 * 16)

    return nc


def _make_tables(w):
    own, oth = _own_oth(w)
    perm = np.concatenate([own, oth])
    pos = perm.astype(np.float64)
    half = 64
    freqs = 10000.0 ** (-np.arange(half, dtype=np.float64) / half)
    rad = pos[None, :] * freqs[:, None]
    cs = np.cos(rad) * MULT
    sn = np.sin(rad) * MULT
    tc = np.concatenate([cs, cs], axis=0).astype(ml_dtypes.bfloat16)
    ts = np.concatenate([-sn, sn], axis=0).astype(ml_dtypes.bfloat16)
    return perm, tc, ts


def _make_masks(w):
    own, oth = _own_oth(w)
    mv = np.zeros((8, 32), dtype=np.float32)
    for i in range(8):
        for (kind, c0, pidx, masked) in p2_slot_pairs(i):
            if kind == "oth":
                mv[i, pidx] = NEG if oth[(c0 - 32) * 128] > own[i * 512] else 0.0
    mval_b = np.ascontiguousarray(
        np.broadcast_to(mv[None], (128, 8, 32)), dtype=np.float32)
    jj = np.arange(896)[None, :]
    pp = np.arange(128)[:, None]
    cm = (pp <= jj - 384).astype(np.float32).astype(ml_dtypes.bfloat16)
    return mval_b, cm


_PROG = None


def kernel(L0_attn_x, L0_attn_wq, L0_attn_wk, L0_attn_wv, L0_attn_wo,
           L0_ffn_w1_w, L0_ffn_w1_s, L0_ffn_w2_w, L0_ffn_w2_s,
           L0_ffn_h1_w, L0_ffn_h1_s, L0_ffn_h2_w, L0_ffn_h2_s,
           L0_ffn_w3_w, L0_ffn_w3_s):
    global _PROG
    x = np.asarray(L0_attn_x, dtype=np.float32)
    wq = np.asarray(L0_attn_wq, dtype=np.float32)
    wk = np.asarray(L0_attn_wk, dtype=np.float32)
    wv = np.asarray(L0_attn_wv, dtype=np.float32)
    wo = np.asarray(L0_attn_wo, dtype=np.float32)

    def blk_qk(wm):
        wt = np.ascontiguousarray(wm.transpose(1, 0, 2).reshape(1024, 1024))
        return np.ascontiguousarray(
            wt.reshape(8, 128, 8, 128).transpose(2, 1, 0, 3))

    wqb, wkb = blk_qk(wq), blk_qk(wk)
    swap = lambda wm: np.concatenate([wm[:, :, 64:], wm[:, :, :64]], axis=2)
    wqsb, wksb = blk_qk(swap(wq)), blk_qk(swap(wk))
    wvb = np.ascontiguousarray(
        wv.transpose(1, 0, 2).reshape(1024, 1024).reshape(8, 128, 1024))
    wob = np.ascontiguousarray(wo.transpose(1, 0, 2))

    def blk_ffn(wm, n_oc, n_ic):
        return np.ascontiguousarray(
            wm.reshape(n_oc, 128, n_ic, 128).transpose(0, 3, 2, 1))

    w1b = blk_ffn(np.asarray(L0_ffn_w1_w, np.float32), 32, 8)
    w2b = blk_ffn(np.asarray(L0_ffn_w2_w, np.float32), 32, 8)
    h1b = blk_ffn(np.asarray(L0_ffn_h1_w, np.float32), 32, 32)
    h2b = blk_ffn(np.asarray(L0_ffn_h2_w, np.float32), 32, 32)
    w3b = np.ascontiguousarray(
        np.asarray(L0_ffn_w3_w, np.float32).T.reshape(32, 128, 1024))
    sc4 = np.concatenate([np.asarray(s, np.float32).reshape(32, 128)
                          for s in (L0_ffn_w1_s, L0_ffn_w2_s,
                                    L0_ffn_h1_s, L0_ffn_h2_s)])
    scT = np.ascontiguousarray(sc4.T)
    w3s = np.ascontiguousarray(np.asarray(L0_ffn_w3_s, np.float32))
    ident = np.eye(128, dtype=ml_dtypes.bfloat16)

    percore = {}
    for w in range(2):
        perm, tca, tsa = _make_tables(w)
        mval_b, cm = _make_masks(w)
        percore[w] = (perm, tca, tsa, mval_b, cm)

    if _PROG is None:
        _PROG = build_program()
    nc = _PROG

    in_maps = []
    for c in range(8):
        b, w = c // 2, c % 2
        perm, tca, tsa, mval_b, cm = percore[w]
        in_maps.append({
            "x": np.ascontiguousarray(x[b][perm]),
            "wqb": wqb, "wkb": wkb, "wqsb": wqsb, "wksb": wksb,
            "wvb": wvb, "wob": wob,
            "w1b": w1b, "w2b": w2b, "h1b": h1b, "h2b": h2b, "w3b": w3b,
            "scT": scT, "w3s": w3s, "tc": tca, "ts": tsa,
            "cm01": cm, "mval": mval_b, "ident": ident,
        })
    res = run_bass_kernel_spmd(nc, in_maps, core_ids=list(range(8)))
    out = np.empty((B, T, DM), dtype=np.float32)
    for c in range(8):
        b, w = c // 2, c % 2
        own, _ = _own_oth(w)
        out[b][own] = res.results[c]["out"]
    return out


# version 09
# speedup vs baseline: 1.0900x; 1.0900x over previous
import numpy as np
import ml_dtypes
from contextlib import ExitStack

import concourse.bass as bass
import concourse.mybir as mybir
from concourse.bass_utils import run_bass_kernel_spmd

F32 = mybir.dt.float32
BF16 = mybir.dt.bfloat16
AF = mybir.ActivationFunctionType
ALU = mybir.AluOpType

B, T, DM = 4, 8192, 1024
H, DH = 8, 128
NQ = 4096
NT = 16
NOWN = 8
MULT = float(DH) ** -0.5
EPS = 1e-5
NEG = -30000.0
C2 = [16, 16, 16, 16, 32, 32, 32, 32]
C2NEW = [16, 0, 0, 0, 16, 0, 0, 0]


def _own_oth(w):
    if w == 0:
        own = np.r_[0:2048, 6144:8192]
        oth = np.r_[2048:6144]
    else:
        own = np.r_[2048:6144]
        oth = np.r_[0:2048, 6144:8192]
    return own, oth


def p2_slot_pairs(i):
    pairs = []
    n_own = 4 * (i + 1)
    pidx = 0
    for p in range(n_own // 2):
        c0 = 2 * p
        pairs.append(("own", c0, pidx, c0 >= n_own - 4))
        pidx += 1
    for p in range(C2[i] // 2):
        pairs.append(("oth", 32 + 2 * p, pidx, False))
        pidx += 1
    return pairs


def p2_structure():
    pair_list, slot_meta = [], {}
    kvl_dma = 0
    for h in range(H):
        for i in range(8):
            n_dma = 2 + (3 if C2NEW[i] else 0)
            kvl_dma += n_dma
            pairs = p2_slot_pairs(i)
            first_gp = len(pair_list)
            for (kind, c0, pidx, masked) in pairs:
                pair_list.append(dict(
                    h=h, i=i, kind=kind, c0=c0, pidx=pidx, masked=masked,
                    gp=len(pair_list), first=(len(pair_list) == first_gp),
                    kvl_after=kvl_dma, last_of_slot=False))
            pair_list[-1]["last_of_slot"] = True
            slot_meta[(h, i)] = dict(first_gp=first_gp,
                                     last_gp=len(pair_list) - 1,
                                     n_slot=h * 8 + i + 1,
                                     kvl_after=kvl_dma)
    n_msk = 0
    for d in pair_list:
        if d["masked"]:
            n_msk += 1
        d["msk_after"] = n_msk
    return pair_list, slot_meta


P2_PAIRS, P2_SLOTS = p2_structure()
NSLOT = 64


def p1_gseq(t):
    seq = [("K", h) for h in range(H)]
    if t < NOWN:
        seq += [("Q", h) for h in range(H)]
    return seq


P1_GCUM = [0]
for _t in range(NT):
    P1_GCUM.append(P1_GCUM[-1] + len(p1_gseq(_t)))

P4_LAYERS = [("w1", 32, 8), ("w2", 32, 8), ("h1", 32, 32), ("h2", 32, 32)]
GR_TILE = 256


def p4_load_schedule():
    sched = []
    grp = 0
    for (lname, n_oc, n_ic) in P4_LAYERS:
        step = 4 if n_ic == 8 else 1
        for o0 in range(0, n_oc, step):
            sched.append((lname, o0, step, grp))
            grp += step
    for pr in range(4):
        for i0 in range(0, 32, 4):
            sched.append(("w3", i0, 4, grp))
            grp += 4
    assert grp == GR_TILE
    return sched


P4_SCHED = p4_load_schedule()
N_LOADS_TILE = len(P4_SCHED)


def build_program(dbg=False, races=True):
    nc = bass.Bass("TRN2", target_bir_lowering=False,
                   detect_race_conditions=races)
    SCR = "ExternalOutput" if dbg else "Internal"

    dx = nc.dram_tensor("x", [T, DM], F32, kind="ExternalInput").ap()
    dwq = nc.dram_tensor("wqb", [8, 128, 8, 128], F32, kind="ExternalInput").ap()
    dwk = nc.dram_tensor("wkb", [8, 128, 8, 128], F32, kind="ExternalInput").ap()
    dwv = nc.dram_tensor("wvb", [8, 128, 1024], F32, kind="ExternalInput").ap()
    dwo = nc.dram_tensor("wob", [128, 8, 1024], F32, kind="ExternalInput").ap()
    dw1 = nc.dram_tensor("w1b", [32, 128, 8, 128], F32, kind="ExternalInput").ap()
    dw2 = nc.dram_tensor("w2b", [32, 128, 8, 128], F32, kind="ExternalInput").ap()
    dh1 = nc.dram_tensor("h1b", [32, 128, 32, 128], F32, kind="ExternalInput").ap()
    dh2 = nc.dram_tensor("h2b", [32, 128, 32, 128], F32, kind="ExternalInput").ap()
    dw3 = nc.dram_tensor("w3b", [32, 128, 1024], F32, kind="ExternalInput").ap()
    dsct = nc.dram_tensor("scT", [128, 128], F32, kind="ExternalInput").ap()
    dw3s = nc.dram_tensor("w3s", [1024], F32, kind="ExternalInput").ap()
    dtc = nc.dram_tensor("tc", [128, T], BF16, kind="ExternalInput").ap()
    dts = nc.dram_tensor("ts", [128, T], BF16, kind="ExternalInput").ap()
    dwqs = nc.dram_tensor("wqsb", [8, 128, 8, 128], F32,
                          kind="ExternalInput").ap()
    dwks = nc.dram_tensor("wksb", [8, 128, 8, 128], F32,
                          kind="ExternalInput").ap()
    dcm = nc.dram_tensor("cm01", [128, 896], BF16, kind="ExternalInput").ap()
    dmv = nc.dram_tensor("mval", [128, 8, 32], F32, kind="ExternalInput").ap()
    did = nc.dram_tensor("ident", [128, 128], BF16, kind="ExternalInput").ap()

    dout = nc.dram_tensor("out", [NQ, DM], F32, kind="ExternalOutput").ap()
    s_kt = nc.dram_tensor("kt_s", [H, 128, T], BF16, kind=SCR).ap()
    s_v = nc.dram_tensor("v_s", [H, T, 128], BF16, kind=SCR).ap()
    s_xp = nc.dram_tensor("xp_s", [NQ, DM], F32, kind=SCR).ap()
    s_x3 = nc.dram_tensor("x3_s", [8, 128, NQ], BF16, kind=SCR).ap()
    dqo = (nc.dram_tensor("qo_d", [128, 8, 4096], BF16,
                          kind="ExternalOutput").ap() if dbg else None)
    dx1 = (nc.dram_tensor("x1t_d", [128, 8, 512], BF16,
                          kind="ExternalOutput").ap() if dbg else None)
    s_w1 = nc.dram_tensor("w1_s", [32, 128, 8, 128], BF16, kind="Internal").ap()
    s_w2 = nc.dram_tensor("w2_s", [32, 128, 8, 128], BF16, kind="Internal").ap()
    s_h1 = nc.dram_tensor("h1_s", [32, 128, 32, 128], BF16, kind="Internal").ap()
    s_h2 = nc.dram_tensor("h2_s", [32, 128, 32, 128], BF16, kind="Internal").ap()
    s_w3 = nc.dram_tensor("w3_s", [32, 128, 1024], BF16, kind="Internal").ap()

    with ExitStack() as ctx:
        ecx = ctx.enter_context

        SB_BASE = 16512
        SB_LIMIT = 212832

        def at(name, off, shape, dtype):
            end = off + int(np.prod(shape[1:])) * mybir.dt.size(dtype)
            assert end <= SB_LIMIT, (name, end)
            return nc.alloc_sbuf_tensor_at(name, shape, dtype,
                                           offset=SB_BASE + off)

        ident = at("identt", 0, [128, 128], BF16)
        ones_c = at("ones_c", 256, [128, 1], BF16)
        ones_r = at("ones_r", 288, [1, 128], BF16)
        eps_t = at("eps_t", 544, [128, 1], F32)
        zero_t = at("zero_t", 576, [128, 1], F32)
        mval = at("mval_sb", 608, [128, 8, 32], F32)
        scT = at("scT_sb", 1632, [128, 128], F32)
        w3sb = at("w3sb", 2144, [128, 1024], F32)
        cm01 = at("cm01_sb", 6240, [128, 896], BF16)
        stt = []
        off = 8032
        for par in range(2):
            d = {}
            d["st6"] = at(f"st6_{par}", off, [128, 12], F32); off += 64
            d["ab"] = at(f"ab_{par}", off, [128, 4], F32); off += 32
            for nm in ("dm", "var", "m", "negm", "rstd", "bln"):
                d[nm] = at(f"{nm}_{par}", off, [128, 1], F32); off += 32
            stt.append(d)
        rrow = [at(f"rrow{p}", off + 1024 * p, [1, 512], BF16) for p in range(2)]
        off += 2048
        spc_a = at("spc_a", off, [128, 128], F32); off += 512
        spc_b = at("spc_b", off, [128, 128], F32); off += 512
        assert off <= 16384

        AR = 16384
        wqb = at("wqb_sb", AR, [128, 8, 8, 128], BF16)
        wkb = at("wkb_sb", AR + 16384, [128, 8, 8, 128], BF16)
        wqs = at("wqs_sb", AR + 32768, [128, 8, 8, 128], BF16)
        wks = at("wks_sb", AR + 49152, [128, 8, 8, 128], BF16)
        wvb = at("wvb_sb", AR + 65536, [128, 8, 1024], BF16)
        xb = [at(f"xb{p}", AR + 81920 + 8192 * p, [128, 4, 1024], BF16)
              for p in range(2)]
        x1s = [at(f"x1s{p}", AR + 98304 + 2048 * p, [128, 1024], BF16)
               for p in range(2)]
        x1t = at("x1t", AR + 102400, [128, 8, 512], BF16)
        krope = at("krope", AR + 110592, [128, 4, 512], BF16)
        vstage = at("vstage", AR + 114688, [128, 4, 1024], BF16)
        rp1 = at("rp1", AR + 122880, [128, 512], BF16)
        rp2 = at("rp2", AR + 123904, [128, 512], BF16)
        tcb = [at(f"tcb{p}", AR + 124928 + 1024 * p, [128, 512], BF16)
               for p in range(2)]
        tsb = [at(f"tsb{p}", AR + 126976 + 1024 * p, [128, 512], BF16)
               for p in range(2)]
        kbuf = [at(f"kbuf{p}", AR + 16384 * p, [128, T], BF16) for p in range(2)]
        vbuf = [at(f"vbuf{p}", AR + 32768 + 16384 * p, [128, 64, 128], BF16)
                for p in range(2)]
        pT = [at(f"pT{p}", AR + 65536 + 2048 * p, [128, 1024], BF16)
              for p in range(2)]
        bcs = at("bcs", AR + 69632, [128, 512], F32)
        P3 = AR + 73728
        wob = at("wob_sb", P3, [128, 8, 1024], BF16)
        xo = [at(f"xo{p}", P3 + 16384 + 4096 * p, [128, 1024], F32)
              for p in range(2)]
        xps = [at(f"xps{p}", P3 + 24576 + 4096 * p, [128, 1024], F32)
               for p in range(2)]
        x3s = [at(f"x3s{p}", P3 + 32768 + 2048 * p, [128, 1024], BF16)
               for p in range(2)]
        x3t = [at(f"x3t{p}", P3 + 36864 + 2048 * p, [128, 8, 128], BF16)
               for p in range(2)]
        QO = at("QO", SB_LIMIT - 65536, [128, 8, 4096], BF16)
        AR4 = AR - 192
        ha = at("ha", AR4, [128, 32, 1024], BF16)
        hb = at("hb", AR4 + 65536, [128, 32, 1024], BF16)
        x3tt = at("x3tt", AR4 + 131072, [128, 8, 1024], BF16)
        ws = [at(f"ws{p}", AR4 + 147456 + 8192 * p, [128, 4096], BF16)
              for p in range(3)]
        xpb = [at(f"xpb{p}", AR4 + 172032 + 4096 * p, [128, 1024], F32)
               for p in range(2)]
        outb = [at(f"outb{p}", AR4 + 180224 + 4096 * p, [128, 1024], F32)
                for p in range(2)]
        f4 = [at(f"f4{p}", AR4 + 188416 + 4096 * p, [128, 1024], F32)
              for p in range(2)]

        pA = ecx(nc.psum_tensor("pA", [128, 1024], F32))
        pB = ecx(nc.psum_tensor("pB", [128, 1024], F32))
        pC = ecx(nc.psum_tensor("pC", [128, 1024], F32))
        pD = ecx(nc.psum_tensor("pD", [128, 1024], F32))
        spair = [pA, pB]

        sem_names = [
            "c_ld", "prep", "x_ld", "sst", "apl", "tps", "tev", "pk", "rk",
            "pv2", "vev", "spk", "spv", "kvl", "smm", "msk", "expd", "spd",
            "rcp", "bce", "nrm", "xol", "pr3", "sv3", "t3d", "xpd", "spxp",
            "sv3b", "ap3", "tp3", "te3", "spx3", "wl4", "x3l", "xpl", "pe4",
            "rsd1", "rsd3", "rsd4", "tbl0", "tbl1", "xld0", "xld1", "kvl0", "kvl1",
            "spk0", "spk1", "spk2", "spk3", "spxp0", "spxp1", "spx30",
            "spx31", "xol0", "xol1", "xpl0", "xpl1", "outd0", "outd1",
            "wl40", "wl41", "wl42",
            "ev4", "f4d", "s4d", "o4d", "outd", "ini",
        ]
        SM = {n: ecx(nc.semaphore(n)) for n in sem_names}
        N_CLD = 11 * 16

        blk = ecx(nc.Block())

        def vsp(ve):
            return ve.tensor_copy(out=spc_b[:], in_=spc_a[:])

        def ssp(sc):
            return sc.activation(out=spc_b[:, 0:64], in_=spc_a[:, 0:64],
                                 func=AF.Copy)

        def stats_dve(ve, d, src_a, src_b, sem):
            ve.bn_stats(out=d["st6"][:, 0:6], in_=src_a)
            ve.bn_stats(out=d["st6"][:, 6:12], in_=src_b)
            vsp(ve)
            ve.bn_aggr(out=d["ab"][:, 0:2], in_=d["st6"][:, 0:12])
            vsp(ve)
            vsp(ve).then_inc(sem, 1)

        def rstd_scalar(sc, d):
            sc.activation(out=d["dm"][:], in_=d["ab"][:, 1:2], func=AF.Ln,
                          bias=eps_t[:, 0:1])
            ssp(sc)
            sc.activation(out=d["rstd"][:], in_=d["dm"][:], func=AF.Exp,
                          scale=-0.5)
            ssp(sc)
            return ssp(sc)

        def ln_apply_dve(ve, d, out_ap, in_ap):
            return ve.tensor_scalar(
                out=out_ap, in0=in_ap,
                scalar1=d["ab"][:, 0:1], scalar2=d["rstd"][:, 0:1],
                op0=ALU.subtract, op1=ALU.mult)

        def p3_tp(te, qc):
            te.wait_ge(SM["ap3"], qc + 1)
            if qc >= 2:
                te.wait_ge(SM["te3"], qc - 1)
            dst = pD.bitcast(BF16)
            half = qc % 2
            for fc in range(8):
                mm = te.transpose(
                    out=dst[:, half * 1024 + fc * 128:half * 1024 + (fc + 1) * 128],
                    in_=x3s[qc % 2][:, fc * 128:(fc + 1) * 128],
                    identity=ident[:])
            mm.then_inc(SM["tp3"], 1)

        def p3_te3(ve, qc):
            ve.wait_ge(SM["tp3"], qc + 1)
            if qc >= 2:
                ve.wait_ge(SM["spx30" if qc % 2 == 0 else "spx31"],
                           16 * (qc // 2))
            half = qc % 2
            ve.tensor_copy(
                out=x3t[qc % 2][:],
                in_=pD.bitcast(BF16)[:, half * 1024:(half + 1) * 1024]
                .rearrange("p (c d) -> p c d", c=8)).then_inc(SM["te3"], 1)

        @blk.gpsimd
        def _(gp):
            gp.dma_start(out=ident[:], in_=did).then_inc(SM["c_ld"], 16)
            gp.dma_start(out=cm01[:], in_=dcm).then_inc(SM["c_ld"], 16)
            gp.dma_start(out=mval[:], in_=dmv).then_inc(SM["c_ld"], 16)
            gp.dma_start(out=scT[:], in_=dsct).then_inc(SM["c_ld"], 16)
            gp.dma_start(out=w3sb[:],
                         in_=dw3s[None, :].partition_broadcast(128)
                         ).then_inc(SM["c_ld"], 16)
            gp.dma_start(out=wqb[:], in_=dwq.rearrange("o p i d -> p o i d")).then_inc(SM["c_ld"], 16)
            gp.dma_start(out=wkb[:], in_=dwk.rearrange("o p i d -> p o i d")).then_inc(SM["c_ld"], 16)
            gp.dma_start(out=wqs[:], in_=dwqs.rearrange("o p i d -> p o i d")).then_inc(SM["c_ld"], 16)
            gp.dma_start(out=wks[:], in_=dwks.rearrange("o p i d -> p o i d")).then_inc(SM["c_ld"], 16)
            gp.dma_start(out=wvb[:], in_=dwv.rearrange("i p m -> p i m")).then_inc(SM["c_ld"], 16)
            def load_x(t):
                gp.dma_start(
                    out=xb[t % 2][:],
                    in_=dx[t * 512:(t + 1) * 512, :].rearrange(
                        "(i p) m -> p i m", p=128),
                ).then_inc(SM["xld0" if t % 2 == 0 else "xld1"], 16)

            load_x(0)
            load_x(1)
            for t in range(NT):
                t0 = t * 512
                if t + 2 < NT:
                    gp.wait_ge(SM["apl"], 4 * (t + 1))
                    load_x(t + 2)
                for h in range(H):
                    gp.wait_ge(SM["rk"], P1_GCUM[t] + h + 1)
                    gk = t * 8 + h
                    gp.dma_start(out=s_kt[h, :, t0:t0 + 512],
                                 in_=krope[:, gk % 4, :]
                                 ).then_inc(SM["spk" + str(gk % 4)], 16)
                gp.wait_ge(SM["vev"], 8 * (t + 1))
                for h in range(H):
                    gp.dma_start(
                        out=s_v[h, t0:t0 + 512, :].rearrange(
                            "(i p) d -> p i d", p=128),
                        in_=vstage[:, :, h * 128:(h + 1) * 128],
                    ).then_inc(SM["spv"], 16)
                if dbg and t == 0:
                    gp.dma_start(out=dx1, in_=x1t[:]).then_inc(SM["s4d"], 16)
            gp.wait_ge(SM["apl"], 64)
            gp.wait_ge(SM["vev"], 8 * NT)
            gp.wait_ge(SM["rk"], P1_GCUM[NT])
            gp.dma_start(out=wob[:], in_=dwo).then_inc(SM["c_ld"], 16)
            gp.dma_start(out=s_w1, in_=dw1).then_inc(SM["prep"], 16)
            gp.dma_start(out=s_w2, in_=dw2).then_inc(SM["prep"], 16)
            gp.dma_start(out=s_h1, in_=dh1).then_inc(SM["prep"], 16)
            gp.dma_start(out=s_h2, in_=dh2).then_inc(SM["prep"], 16)
            gp.dma_start(out=s_w3, in_=dw3).then_inc(SM["prep"], 16)
            if dbg:
                gp.wait_ge(SM["nrm"], NSLOT)
                gp.dma_start(out=dqo, in_=QO[:]).then_inc(SM["s4d"], 16)

        @blk.sync
        def _(sy):
            for t in range(NT):
                if t >= 2:
                    sy.wait_ge(SM["rk"], P1_GCUM[t - 1])
                tb = "tbl0" if t % 2 == 0 else "tbl1"
                sy.dma_start(out=tcb[t % 2][:],
                             in_=dtc[:, t * 512:(t + 1) * 512]
                             ).then_inc(SM[tb], 16)
                sy.dma_start(out=tsb[t % 2][:],
                             in_=dts[:, t * 512:(t + 1) * 512]
                             ).then_inc(SM[tb], 16)
            for _r in range(4):
                sy.wait_ge(SM["spk" + str(_r)], 16 * 32)
            sy.wait_ge(SM["spv"], 16 * NT * H)
            for h in range(H):
                if h >= 2:
                    sy.wait_ge(SM["nrm"], 8 * (h - 1))
                kb, vb = kbuf[h % 2], vbuf[h % 2]
                kvs = "kvl0" if h % 2 == 0 else "kvl1"
                for i in range(8):
                    a, b = 4 * i * 128, 4 * (i + 1) * 128
                    sy.dma_start(out=kb[:, a:b],
                                 in_=s_kt[h, :, a:b]).then_inc(SM[kvs], 16)
                    sy.dma_start(
                        out=vb[:, 4 * i:4 * (i + 1), :],
                        in_=s_v[h, a:b, :].rearrange("(c p) d -> p c d", p=128),
                    ).then_inc(SM[kvs], 16)
                    if C2NEW[i]:
                        oc0 = 32 + (0 if i == 0 else 16)
                        ga, gb = oc0 * 128, (oc0 + 16) * 128
                        sy.dma_start(out=kb[:, ga:gb],
                                     in_=s_kt[h, :, ga:gb]
                                     ).then_inc(SM[kvs], 16)
                        for half in range(2):
                            c0 = oc0 + 8 * half
                            sy.dma_start(
                                out=vb[:, c0:c0 + 8, :],
                                in_=s_v[h, c0 * 128:(c0 + 8) * 128, :]
                                .rearrange("(c p) d -> p c d", p=128),
                            ).then_inc(SM[kvs], 16)
            for qc in range(32):
                if qc >= 2:
                    sy.wait_ge(SM["xpd"], qc - 1)
                sy.dma_start(out=xo[qc % 2][:],
                             in_=dx[qc * 128:(qc + 1) * 128, :]
                             ).then_inc(SM["xol0" if qc % 2 == 0 else "xol1"],
                                        16)
            sy.wait_ge(SM["nrm"], NSLOT)
            sy.wait_ge(SM["spx30"], 16 * 16)
            sy.wait_ge(SM["spx31"], 16 * 16)
            sy.wait_ge(SM["spxp0"], 16 * 16)
            sy.wait_ge(SM["spxp1"], 16 * 16)
            scr = {"w1": s_w1, "w2": s_w2, "h1": s_h1, "h2": s_h2}
            sy.wait_ge(SM["prep"], 16 * 5)
            lg = 0
            for t4 in range(4):
                if t4 >= 1:
                    sy.wait_ge(SM["pe4"], GR_TILE * (t4 - 1) + 64)
                sy.dma_start(
                    out=x3tt[:],
                    in_=s_x3[:, :, t4 * 1024:(t4 + 1) * 1024].rearrange(
                        "c p t -> p c t"),
                ).then_inc(SM["x3l"], 16)
                for li, (lname, o0, cnt, grp) in enumerate(P4_SCHED):
                    if lg >= 3:
                        if li >= 3:
                            pl, po, pc, pg = P4_SCHED[li - 3]
                            gate = GR_TILE * t4 + pg + pc
                        else:
                            pl, po, pc, pg = P4_SCHED[li - 3 + N_LOADS_TILE]
                            gate = GR_TILE * (t4 - 1) + pg + pc
                        sy.wait_ge(SM["pe4"], gate)
                    if lname == "w3":
                        src = s_w3[o0:o0 + cnt].rearrange("i p m -> p i m")
                        dst = ws[lg % 3][:, 0:cnt * 1024].rearrange(
                            "p (i m) -> p i m", i=cnt)
                    elif lname in ("w1", "w2"):
                        src = scr[lname][o0:o0 + cnt].rearrange(
                            "o p i d -> p o i d")
                        dst = ws[lg % 3][:, 0:cnt * 1024].rearrange(
                            "p (o i d) -> p o i d", o=cnt, i=8)
                    else:
                        src = scr[lname][o0].rearrange("p i d -> p i d")
                        dst = ws[lg % 3][:, 0:4096].rearrange(
                            "p (i d) -> p i d", i=32)
                    sy.dma_start(out=dst,
                                 in_=src).then_inc(SM["wl4" + str(lg % 3)], 16)
                    lg += 1
                    if lname == "w3" and o0 == 28:
                        pr = (grp - 128) // 32
                        for k in range(2):
                            qg = t4 * 8 + pr * 2 + k
                            if qg >= 2:
                                sy.wait_ge(SM["o4d"], qg - 1)
                            sy.dma_start(out=xpb[k][:],
                                         in_=s_xp[qg * 128:(qg + 1) * 128, :]
                                         ).then_inc(
                                SM["xpl0" if k == 0 else "xpl1"], 16)

        @blk.tensor
        def _(te):
            te.wait_ge(SM["c_ld"], N_CLD - 16)
            te.wait_ge(SM["ini"], 1)
            pkc = 0
            for t in range(NT):
                for i in range(4):
                    te.wait_ge(SM["apl"], 4 * t + i + 1)
                    if 4 * t + i >= 2:
                        te.wait_ge(SM["tev"], 4 * t + i - 1)
                    dst = pD.bitcast(BF16)
                    half = (4 * t + i) % 2
                    for fc in range(8):
                        mm = te.transpose(
                            out=dst[:, half * 1024 + fc * 128:
                                    half * 1024 + (fc + 1) * 128],
                            in_=x1s[i % 2][:, fc * 128:(fc + 1) * 128],
                            identity=ident[:])
                    mm.then_inc(SM["tps"], 1)
                te.wait_ge(SM["tev"], 4 * (t + 1))
                for (kind, h) in p1_gseq(t):
                    wb = wkb if kind == "K" else wqb
                    wbs = wks if kind == "K" else wqs
                    bank = pA if pkc % 2 == 0 else pB
                    if pkc >= 2:
                        te.wait_ge(SM["rk"], pkc - 1)
                    for ic in range(8):
                        te.matmul(bank[:, 0:512], wb[:, h, ic, :],
                                  x1t[:, ic, :],
                                  start=(ic == 0), stop=(ic == 7))
                        mm = te.matmul(bank[:, 512:1024], wbs[:, h, ic, :],
                                       x1t[:, ic, :],
                                       start=(ic == 0), stop=(ic == 7))
                    mm.then_inc(SM["pk"], 1)
                    pkc += 1
                for i2 in range(4):
                    for nh in range(2):
                        v = 8 * t + 2 * i2 + nh
                        if v >= 2:
                            te.wait_ge(SM["vev"], v - 1)
                        for ic in range(8):
                            mm = te.matmul(
                                pC[:, nh * 512:(nh + 1) * 512],
                                x1t[:, ic, i2 * 128:(i2 + 1) * 128],
                                wvb[:, ic, nh * 512:(nh + 1) * 512],
                                start=(ic == 0), stop=(ic == 7))
                        mm.then_inc(SM["pv2"], 1)
            def emit_S(d):
                if d["first"]:
                    te.wait_ge(SM["kvl0" if d["h"] % 2 == 0 else "kvl1"],
                               16 * 22 * (d["h"] // 2 + 1))
                gp_ = d["gp"]
                if gp_ >= 2:
                    te.wait_ge(SM["expd"], gp_ - 1)
                sp = spair[gp_ % 2]
                kb = kbuf[d["h"] % 2]
                qslice = QO[:, d["h"], d["i"] * 512:(d["i"] + 1) * 512]
                for k in range(2):
                    c = d["c0"] + k
                    mm = te.matmul(sp[:, k * 512:(k + 1) * 512],
                                   kb[:, c * 128:(c + 1) * 128], qslice,
                                   start=True, stop=True)
                mm.then_inc(SM["smm"], 1)

            def emit_SP(d):
                gp_ = d["gp"]
                sm = P2_SLOTS[(d["h"], d["i"])]
                if d["masked"]:
                    te.wait_ge(SM["msk"], d["msk_after"])
                else:
                    te.wait_ge(SM["expd"], gp_ + 1)
                if d["first"] and sm["n_slot"] >= 2:
                    te.wait_ge(SM["rcp"], sm["n_slot"] - 1)
                    te.wait_ge(SM["nrm"], sm["n_slot"] - 1)
                vb = vbuf[d["h"] % 2]
                pt = pT[gp_ % 2]
                for k in range(2):
                    c = d["c0"] + k
                    st = d["first"] and k == 0
                    sp_ = d["last_of_slot"] and k == 1
                    te.matmul(pC[0:1, 0:512], ones_c[:],
                              pt[:, k * 512:(k + 1) * 512], start=st, stop=sp_)
                    mm = te.matmul(pD[:, 0:512], vb[:, c, :],
                                   pt[:, k * 512:(k + 1) * 512],
                                   start=st, stop=sp_)
                mm.then_inc(SM["spd"], 1)

            def emit_bcast(d):
                sm = P2_SLOTS[(d["h"], d["i"])]
                te.wait_ge(SM["rcp"], sm["n_slot"])
                te.matmul(pC[:, 512:1024], ones_r[:],
                          rrow[(sm["n_slot"] - 1) % 2][:],
                          start=True, stop=True).then_inc(SM["bce"], 1)

            prev = None
            pending_bcast = None
            for d in P2_PAIRS:
                if prev is not None and prev["last_of_slot"]:
                    emit_SP(prev)
                    pending_bcast = prev
                    prev = None
                emit_S(d)
                if pending_bcast is not None:
                    emit_bcast(pending_bcast)
                    pending_bcast = None
                if prev is not None:
                    emit_SP(prev)
                prev = d
            emit_SP(prev)
            emit_bcast(prev)
            te.wait_ge(SM["nrm"], NSLOT)
            te.wait_ge(SM["c_ld"], N_CLD)
            for qc in range(32):
                dst = pA if qc % 2 == 0 else pB
                if qc >= 2:
                    te.wait_ge(SM["t3d"], qc - 1)
                for h in range(H):
                    for nh in range(2):
                        mm = te.matmul(dst[:, nh * 512:(nh + 1) * 512],
                                       QO[:, h, qc * 128:(qc + 1) * 128],
                                       wob[:, h, nh * 512:(nh + 1) * 512],
                                       start=(h == 0), stop=(h == 7))
                mm.then_inc(SM["pr3"], 1)
                if qc >= 1:
                    p3_tp(te, qc - 1)
            p3_tp(te, 31)
            ev4_base = 0
            lgc = 0
            for t4 in range(4):
                te.wait_ge(SM["x3l"], 16 * (t4 + 1))
                if t4 >= 1:
                    te.wait_ge(SM["f4d"], 8 * t4)
                ev4_seen = 0
                for (lname, o0, cnt, grp) in P4_SCHED:
                    te.wait_ge(SM["wl4" + str(lgc % 3)], 16 * (lgc // 3 + 1))
                    lgc += 1
                    if lname == "w3":
                        pr = (grp - 128) // 32
                        pe_pair = [pA, pB] if pr % 2 == 0 else [pC, pD]
                        if o0 == 0:
                            if pr == 0:
                                te.wait_ge(SM["ev4"], ev4_base + 128)
                            if pr >= 2:
                                te.wait_ge(SM["f4d"],
                                           8 * t4 + (pr - 2) * 2 + 2)
                        for j in range(cnt):
                            ic = o0 + j
                            for k in range(2):
                                i3 = pr * 2 + k
                                for nh in range(2):
                                    mm = te.matmul(
                                        pe_pair[k][:, nh * 512:(nh + 1) * 512],
                                        hb[:, ic, i3 * 128:(i3 + 1) * 128],
                                        ws[(lgc - 1) % 3]
                                        [:, (j * 8 + nh * 4) * 128:
                                         (j * 8 + nh * 4 + 4) * 128]
                                        if False else
                                        ws[(lgc - 1) % 3]
                                        [:, j * 1024 + nh * 512:
                                         j * 1024 + (nh + 1) * 512],
                                        start=(ic == 0), stop=(ic == 31))
                            mm.then_inc(SM["pe4"], 1)
                        continue
                    if lname == "h1" and o0 == 0:
                        te.wait_ge(SM["ev4"], ev4_base + 64)
                    if lname == "h2" and o0 == 0:
                        te.wait_ge(SM["ev4"], ev4_base + 96)
                    src = x3tt if lname in ("w1", "w2") else (
                        hb if lname == "h1" else ha)
                    n_ic = 8 if lname in ("w1", "w2") else 32
                    for j in range(cnt):
                        oc = o0 + j
                        dstp = pA if oc % 2 == 0 else pB
                        if ev4_seen >= 2:
                            te.wait_ge(SM["ev4"], ev4_base + ev4_seen - 1)
                        for ic in range(n_ic):
                            lhs = ws[(lgc - 1) % 3][
                                :, (j * n_ic + ic) * 128:
                                (j * n_ic + ic + 1) * 128]
                            te.matmul(dstp[:, 0:512], lhs, src[:, ic, 0:512],
                                      start=(ic == 0), stop=(ic == n_ic - 1))
                            mm = te.matmul(dstp[:, 512:1024], lhs,
                                           src[:, ic, 512:1024],
                                           start=(ic == 0),
                                           stop=(ic == n_ic - 1))
                        mm.then_inc(SM["pe4"], 1)
                        ev4_seen += 1
                ev4_base += 128

        @blk.vector
        def _(ve):
            ve.memset(ones_c[:], 1.0)
            ve.memset(ones_r[:], 1.0)
            ve.memset(eps_t[:], EPS)
            ve.memset(zero_t[:], 0.0)
            ve.memset(spc_a[:], 0.0)
            ve.memset(spc_b[:], 0.0)
            vsp(ve)
            vsp(ve).then_inc(SM["ini"], 1)
            ve.wait_ge(SM["c_ld"], N_CLD - 16)
            def p1_stats(ti):
                t, i = ti // 4, ti % 4
                d = stt[i % 2]
                ve.wait_ge(SM["xld0" if t % 2 == 0 else "xld1"],
                           16 * (t // 2 + 1))
                if ti >= 2:
                    ve.wait_ge(SM["apl"], ti - 1)
                src = xb[t % 2][:, i, :]
                stats_dve(ve, d, src[0:128, 0:512], src[0:128, 512:1024],
                          SM["sst"])

            p1_stats(0)
            for t in range(NT):
                for i in range(4):
                    d = stt[i % 2]
                    if 4 * t + i + 1 < 64:
                        p1_stats(4 * t + i + 1)
                    src = xb[t % 2][:, i, :]
                    ve.wait_ge(SM["rsd1"], 4 * t + i + 1)
                    if 4 * t + i >= 2:
                        ve.wait_ge(SM["tps"], 4 * t + i - 1)
                    ln_apply_dve(ve, d, x1s[i % 2][:], src
                                 ).then_inc(SM["apl"], 1)
                    if i >= 1:
                        ti = 4 * t + i - 1
                        ve.wait_ge(SM["tps"], ti + 1)
                        half = ti % 2
                        ve.tensor_copy(
                            out=x1t[:, :, (ti % 4) * 128:(ti % 4) * 128 + 128],
                            in_=pD.bitcast(BF16)[:, half * 1024:
                                                 (half + 1) * 1024]
                            .rearrange("p (c d) -> p c d", c=8)
                        ).then_inc(SM["tev"], 1)
                ti = 4 * t + 3
                ve.wait_ge(SM["tps"], ti + 1)
                half = ti % 2
                ve.tensor_copy(
                    out=x1t[:, :, 384:512],
                    in_=pD.bitcast(BF16)[:, half * 1024:(half + 1) * 1024]
                    .rearrange("p (c d) -> p c d", c=8)).then_inc(SM["tev"], 1)
                ve.wait_ge(SM["tbl0" if t % 2 == 0 else "tbl1"],
                           32 * (t // 2 + 1))
                for gi, (kind, h) in enumerate(p1_gseq(t)):
                    g = P1_GCUM[t] + gi
                    ve.wait_ge(SM["pk"], g + 1)
                    if kind == "K":
                        gk = t * 8 + h
                        if gk >= 4:
                            ve.wait_ge(SM["spk" + str(gk % 4)],
                                       16 * (gk // 4))
                    bank = pA if g % 2 == 0 else pB
                    tt0 = t * 512
                    ve.tensor_tensor(out=rp1[:], in0=bank[:, 0:512],
                                     in1=tcb[t % 2][:], op=ALU.mult)
                    ve.tensor_tensor(out=rp2[:], in0=bank[:, 512:1024],
                                     in1=tsb[t % 2][:], op=ALU.mult)
                    if kind == "K":
                        dst = krope[:, (t * 8 + h) % 4, :]
                    else:
                        dst = QO[:, h, tt0:tt0 + 512]
                    ve.tensor_tensor(out=dst, in0=rp1[:], in1=rp2[:],
                                     op=ALU.add).then_inc(SM["rk"], 1)
            for d in P2_PAIRS:
                gp_ = d["gp"]
                if d["masked"]:
                    ve.wait_ge(SM["expd"], gp_ + 1)
                    pt = pT[gp_ % 2]
                    n_own = 4 * (d["i"] + 1)
                    for k in range(2):
                        c = d["c0"] + k
                        r = c - (n_own - 4)
                        s = 384 - 128 * r
                        mm = ve.tensor_tensor(
                            out=pt[:, k * 512:(k + 1) * 512],
                            in0=pt[:, k * 512:(k + 1) * 512],
                            in1=cm01[:, s:s + 512], op=ALU.mult)
                    mm.then_inc(SM["msk"], 1)
                if d["last_of_slot"]:
                    sm = P2_SLOTS[(d["h"], d["i"])]
                    ve.wait_ge(SM["spd"], sm["last_gp"] + 1)
                    with nc.allow_low_precision(
                            reason="softmax 1/sum as bf16 matmul operand"):
                        ve.reciprocal(out=rrow[(sm["n_slot"] - 1) % 2][:],
                                      in_=pC[0:1, 0:512])
                    vsp(ve).then_inc(SM["rcp"], 1)
                    ve.wait_ge(SM["bce"], sm["n_slot"])
                    ve.tensor_copy(out=bcs[:], in_=pC[:, 512:1024])
                    ve.tensor_tensor(
                        out=QO[:, d["h"], d["i"] * 512:(d["i"] + 1) * 512],
                        in0=pD[:, 0:512], in1=bcs[:], op=ALU.mult
                    ).then_inc(SM["nrm"], 1)
            for qc in range(32):
                par = qc % 2
                d = stt[par]
                srcp = pA if par == 0 else pB
                ve.wait_ge(SM["pr3"], qc + 1)
                stats_dve(ve, d, srcp[:, 0:512], srcp[:, 512:1024], SM["sv3"])
                ve.wait_ge(SM["rsd3"], 2 * qc + 1)
                if qc >= 2:
                    ve.wait_ge(SM["spxp0" if qc % 2 == 0 else "spxp1"],
                               16 * (qc // 2))
                ln_apply_dve(ve, d, xps[par][:], srcp[:]
                             ).then_inc(SM["t3d"], 1)
                ve.wait_ge(SM["xol0" if qc % 2 == 0 else "xol1"],
                           16 * (qc // 2 + 1))
                ve.tensor_add(xps[par][:], xps[par][:], xo[par][:]
                              ).then_inc(SM["xpd"], 1)
                stats_dve(ve, d, xps[par][:, 0:512], xps[par][:, 512:1024],
                          SM["sv3b"])
                ve.wait_ge(SM["rsd3"], 2 * qc + 2)
                if qc >= 2:
                    ve.wait_ge(SM["tp3"], qc - 1)
                ln_apply_dve(ve, d, x3s[par][:], xps[par][:]
                             ).then_inc(SM["ap3"], 1)
                if qc >= 1:
                    p3_te3(ve, qc - 1)
                if qc == 31:
                    p3_te3(ve, 31)
            pe4c = 0
            for t4 in range(4):
                for (lname, n_oc, n_ic) in P4_LAYERS:
                    for oc in range(n_oc):
                        if lname == "w2":
                            ve.wait_ge(SM["pe4"], pe4c + 1)
                            srcp = pA if oc % 2 == 0 else pB
                            ve.scalar_tensor_tensor(
                                out=hb[:, oc, :], in0=srcp[:],
                                scalar=scT[:, 32 + oc:33 + oc],
                                in1=ha[:, oc, :], op0=ALU.mult, op1=ALU.mult
                            ).then_inc(SM["ev4"], 1)
                        pe4c += 1
                pe4c += 128
                for i3 in range(8):
                    qg = t4 * 8 + i3
                    par = i3 % 2
                    d = stt[par]
                    pr, k = i3 // 2, i3 % 2
                    srcp = ([pA, pB] if pr % 2 == 0 else [pC, pD])[k]
                    ve.wait_ge(SM["pe4"], GR_TILE * t4 + 128 + 32 * (pr + 1))
                    ve.tensor_tensor(out=f4[par][:], in0=srcp[:], in1=w3sb[:],
                                     op=ALU.mult).then_inc(SM["f4d"], 1)
                    stats_dve(ve, d, f4[par][:, 0:512], f4[par][:, 512:1024],
                              SM["sv3"])
                    ve.wait_ge(SM["rsd4"], qg + 1)
                    if qg >= 2:
                        ve.wait_ge(SM["outd0" if qg % 2 == 0 else "outd1"],
                                   16 * (qg // 2))
                    ln_apply_dve(ve, d, outb[par][:], f4[par][:])
                    ve.wait_ge(SM["xpl0" if i3 % 2 == 0 else "xpl1"],
                               16 * (qg // 2 + 1))
                    ve.tensor_add(outb[par][:], outb[par][:], xpb[par][:]
                                  ).then_inc(SM["o4d"], 1)

        @blk.scalar
        def _(sc):
            sc.wait_ge(SM["ini"], 1)
            for t in range(NT):
                for i in range(4):
                    d = stt[i % 2]
                    sc.wait_ge(SM["sst"], 4 * t + i + 1)
                    rstd_scalar(sc, d).then_inc(SM["rsd1"], 1)
                for i2 in range(4):
                    for nh in range(2):
                        v = 8 * t + 2 * i2 + nh
                        sc.wait_ge(SM["pv2"], v + 1)
                        if t >= 1 and i2 == 0 and nh == 0:
                            sc.wait_ge(SM["spv"], 16 * 8 * t)
                        sc.activation(
                            out=vstage[:, i2, nh * 512:(nh + 1) * 512],
                            in_=pC[:, nh * 512:(nh + 1) * 512],
                            func=AF.Copy).then_inc(SM["vev"], 1)
            for d in P2_PAIRS:
                gp_ = d["gp"]
                sc.wait_ge(SM["smm"], gp_ + 1)
                if gp_ >= 2:
                    sc.wait_ge(SM["spd"], gp_ - 1)
                bias = mval[:, d["i"], d["pidx"]:d["pidx"] + 1]
                sc.activation(out=pT[gp_ % 2][:], in_=spair[gp_ % 2][:],
                              func=AF.Exp, bias=bias).then_inc(SM["expd"], 1)
            for qc in range(32):
                par = qc % 2
                d = stt[par]
                sc.wait_ge(SM["sv3"], qc + 1)
                rstd_scalar(sc, d).then_inc(SM["rsd3"], 1)
                sc.wait_ge(SM["sv3b"], qc + 1)
                rstd_scalar(sc, d).then_inc(SM["rsd3"], 1)
                sc.wait_ge(SM["xpd"], qc + 1)
                sc.dma_start(out=s_xp[qc * 128:(qc + 1) * 128, :],
                             in_=xps[par][:]
                             ).then_inc(SM["spxp0" if par == 0 else "spxp1"],
                                        16)
                if qc >= 1:
                    sc.wait_ge(SM["te3"], qc)
                    sc.dma_start(out=s_x3[:, :, (qc - 1) * 128:qc * 128]
                                 .rearrange("c p d -> p c d"),
                                 in_=x3t[(qc - 1) % 2][:]
                                 ).then_inc(
                        SM["spx30" if (qc - 1) % 2 == 0 else "spx31"], 16)
            sc.wait_ge(SM["te3"], 32)
            sc.dma_start(out=s_x3[:, :, 31 * 128:32 * 128]
                         .rearrange("c p d -> p c d"),
                         in_=x3t[31 % 2][:]).then_inc(SM["spx31"], 16)
            pe4c = 0
            for t4 in range(4):
                for (lname, n_oc, n_ic) in P4_LAYERS:
                    for oc in range(n_oc):
                        if lname != "w2":
                            sc.wait_ge(SM["pe4"], pe4c + 1)
                            srcp = pA if oc % 2 == 0 else pB
                            dst = ha[:, oc, :] if lname in ("w1", "h1") \
                                else hb[:, oc, :]
                            fn = AF.Silu if lname == "w1" else AF.Gelu
                            col = {"w1": 0, "h1": 64, "h2": 96}[lname] + oc
                            sc.activation(out=dst, in_=srcp[:], func=fn,
                                          scale=scT[:, col:col + 1]
                                          ).then_inc(SM["ev4"], 1)
                        pe4c += 1
                pe4c += 128
                for i3 in range(8):
                    qg = t4 * 8 + i3
                    par = i3 % 2
                    d = stt[par]
                    sc.wait_ge(SM["sv3"], 32 + qg + 1)
                    rstd_scalar(sc, d).then_inc(SM["rsd4"], 1)
                    sc.wait_ge(SM["o4d"], qg + 1)
                    sc.dma_start(out=dout[qg * 128:(qg + 1) * 128, :],
                                 in_=outb[par][:]
                                 ).then_inc(
                        SM["outd0" if par == 0 else "outd1"], 16)
            sc.wait_ge(SM["outd0"], 16 * 16)
            sc.wait_ge(SM["outd1"], 16 * 16)

    return nc


def _make_tables(w):
    own, oth = _own_oth(w)
    perm = np.concatenate([own, oth])
    pos = perm.astype(np.float64)
    half = 64
    freqs = 10000.0 ** (-np.arange(half, dtype=np.float64) / half)
    rad = pos[None, :] * freqs[:, None]
    cs = np.cos(rad) * MULT
    sn = np.sin(rad) * MULT
    tc = np.concatenate([cs, cs], axis=0).astype(ml_dtypes.bfloat16)
    ts = np.concatenate([-sn, sn], axis=0).astype(ml_dtypes.bfloat16)
    return perm, tc, ts


def _make_masks(w):
    own, oth = _own_oth(w)
    mv = np.zeros((8, 32), dtype=np.float32)
    for i in range(8):
        for (kind, c0, pidx, masked) in p2_slot_pairs(i):
            if kind == "oth":
                mv[i, pidx] = NEG if oth[(c0 - 32) * 128] > own[i * 512] else 0.0
    mval_b = np.ascontiguousarray(
        np.broadcast_to(mv[None], (128, 8, 32)), dtype=np.float32)
    jj = np.arange(896)[None, :]
    pp = np.arange(128)[:, None]
    cm = (pp <= jj - 384).astype(np.float32).astype(ml_dtypes.bfloat16)
    return mval_b, cm


_PROG = None


def kernel(L0_attn_x, L0_attn_wq, L0_attn_wk, L0_attn_wv, L0_attn_wo,
           L0_ffn_w1_w, L0_ffn_w1_s, L0_ffn_w2_w, L0_ffn_w2_s,
           L0_ffn_h1_w, L0_ffn_h1_s, L0_ffn_h2_w, L0_ffn_h2_s,
           L0_ffn_w3_w, L0_ffn_w3_s):
    global _PROG
    x = np.asarray(L0_attn_x, dtype=np.float32)
    wq = np.asarray(L0_attn_wq, dtype=np.float32)
    wk = np.asarray(L0_attn_wk, dtype=np.float32)
    wv = np.asarray(L0_attn_wv, dtype=np.float32)
    wo = np.asarray(L0_attn_wo, dtype=np.float32)

    def blk_qk(wm):
        wt = np.ascontiguousarray(wm.transpose(1, 0, 2).reshape(1024, 1024))
        return np.ascontiguousarray(
            wt.reshape(8, 128, 8, 128).transpose(2, 1, 0, 3))

    wqb, wkb = blk_qk(wq), blk_qk(wk)
    swap = lambda wm: np.concatenate([wm[:, :, 64:], wm[:, :, :64]], axis=2)
    wqsb, wksb = blk_qk(swap(wq)), blk_qk(swap(wk))
    wvb = np.ascontiguousarray(
        wv.transpose(1, 0, 2).reshape(1024, 1024).reshape(8, 128, 1024))
    wob = np.ascontiguousarray(wo.transpose(1, 0, 2))

    def blk_ffn(wm, n_oc, n_ic):
        return np.ascontiguousarray(
            wm.reshape(n_oc, 128, n_ic, 128).transpose(0, 3, 2, 1))

    w1b = blk_ffn(np.asarray(L0_ffn_w1_w, np.float32), 32, 8)
    w2b = blk_ffn(np.asarray(L0_ffn_w2_w, np.float32), 32, 8)
    h1b = blk_ffn(np.asarray(L0_ffn_h1_w, np.float32), 32, 32)
    h2b = blk_ffn(np.asarray(L0_ffn_h2_w, np.float32), 32, 32)
    w3b = np.ascontiguousarray(
        np.asarray(L0_ffn_w3_w, np.float32).T.reshape(32, 128, 1024))
    sc4 = np.concatenate([np.asarray(s, np.float32).reshape(32, 128)
                          for s in (L0_ffn_w1_s, L0_ffn_w2_s,
                                    L0_ffn_h1_s, L0_ffn_h2_s)])
    scT = np.ascontiguousarray(sc4.T)
    w3s = np.ascontiguousarray(np.asarray(L0_ffn_w3_s, np.float32))
    ident = np.eye(128, dtype=ml_dtypes.bfloat16)

    percore = {}
    for w in range(2):
        perm, tca, tsa = _make_tables(w)
        mval_b, cm = _make_masks(w)
        percore[w] = (perm, tca, tsa, mval_b, cm)

    if _PROG is None:
        _PROG = build_program()
    nc = _PROG

    in_maps = []
    for c in range(8):
        b, w = c // 2, c % 2
        perm, tca, tsa, mval_b, cm = percore[w]
        in_maps.append({
            "x": np.ascontiguousarray(x[b][perm]),
            "wqb": wqb, "wkb": wkb, "wqsb": wqsb, "wksb": wksb,
            "wvb": wvb, "wob": wob,
            "w1b": w1b, "w2b": w2b, "h1b": h1b, "h2b": h2b, "w3b": w3b,
            "scT": scT, "w3s": w3s, "tc": tca, "ts": tsa,
            "cm01": cm, "mval": mval_b, "ident": ident,
        })
    res = run_bass_kernel_spmd(nc, in_maps, core_ids=list(range(8)))
    out = np.empty((B, T, DM), dtype=np.float32)
    for c in range(8):
        b, w = c // 2, c % 2
        own, _ = _own_oth(w)
        out[b][own] = res.results[c]["out"]
    return out


# version 10
# speedup vs baseline: 1.1567x; 1.0612x over previous
import numpy as np
import ml_dtypes
from contextlib import ExitStack

import concourse.bass as bass
import concourse.mybir as mybir
from concourse.bass_utils import run_bass_kernel_spmd

F32 = mybir.dt.float32
BF16 = mybir.dt.bfloat16
AF = mybir.ActivationFunctionType
ALU = mybir.AluOpType

B, T, DM = 4, 8192, 1024
H, DH = 8, 128
NQ = 4096
NT = 16
NOWN = 8
MULT = float(DH) ** -0.5
EPS = 1e-5
NEG = -30000.0
C2 = [16, 16, 16, 16, 32, 32, 32, 32]
C2NEW = [16, 0, 0, 0, 16, 0, 0, 0]


def _own_oth(w):
    if w == 0:
        own = np.r_[0:2048, 6144:8192]
        oth = np.r_[2048:6144]
    else:
        own = np.r_[2048:6144]
        oth = np.r_[0:2048, 6144:8192]
    return own, oth


def p2_slot_pairs(i):
    pairs = []
    n_own = 4 * (i + 1)
    pidx = 0
    for p in range(n_own // 2):
        c0 = 2 * p
        pairs.append(("own", c0, pidx, c0 >= n_own - 4))
        pidx += 1
    for p in range(C2[i] // 2):
        pairs.append(("oth", 32 + 2 * p, pidx, False))
        pidx += 1
    return pairs


def p2_structure():
    pair_list, slot_meta = [], {}
    kvl_dma = 0
    for h in range(H):
        for i in range(8):
            n_dma = 2 + (3 if C2NEW[i] else 0)
            kvl_dma += n_dma
            pairs = p2_slot_pairs(i)
            first_gp = len(pair_list)
            for (kind, c0, pidx, masked) in pairs:
                pair_list.append(dict(
                    h=h, i=i, kind=kind, c0=c0, pidx=pidx, masked=masked,
                    gp=len(pair_list), first=(len(pair_list) == first_gp),
                    kvl_after=kvl_dma, last_of_slot=False))
            pair_list[-1]["last_of_slot"] = True
            slot_meta[(h, i)] = dict(first_gp=first_gp,
                                     last_gp=len(pair_list) - 1,
                                     n_slot=h * 8 + i + 1,
                                     kvl_after=kvl_dma)
    n_msk = 0
    for d in pair_list:
        if d["masked"]:
            n_msk += 1
        d["msk_after"] = n_msk
    return pair_list, slot_meta


P2_PAIRS, P2_SLOTS = p2_structure()
NSLOT = 64


def p1_gseq(t):
    seq = [("K", h) for h in range(H)]
    if t < NOWN:
        seq += [("Q", h) for h in range(H)]
    return seq


P1_GCUM = [0]
for _t in range(NT):
    P1_GCUM.append(P1_GCUM[-1] + len(p1_gseq(_t)))

P4_LAYERS = [("w1", 32, 8), ("w2", 32, 8), ("h1", 32, 32), ("h2", 32, 32)]
GR_TILE = 256


def p4_load_schedule():
    sched = []
    grp = 0
    for (lname, n_oc, n_ic) in P4_LAYERS:
        step = 4 if n_ic == 8 else 1
        for o0 in range(0, n_oc, step):
            sched.append((lname, o0, step, grp))
            grp += step
    for pr in range(4):
        for i0 in range(0, 32, 4):
            sched.append(("w3", i0, 4, grp))
            grp += 4
    assert grp == GR_TILE
    return sched


P4_SCHED = p4_load_schedule()
N_LOADS_TILE = len(P4_SCHED)


def build_program(dbg=False, races=True):
    nc = bass.Bass("TRN2", target_bir_lowering=False,
                   detect_race_conditions=races)
    SCR = "ExternalOutput" if dbg else "Internal"

    dx = nc.dram_tensor("x", [T, DM], F32, kind="ExternalInput").ap()
    dwq = nc.dram_tensor("wqb", [8, 128, 8, 128], F32, kind="ExternalInput").ap()
    dwk = nc.dram_tensor("wkb", [8, 128, 8, 128], F32, kind="ExternalInput").ap()
    dwv = nc.dram_tensor("wvb", [8, 128, 1024], F32, kind="ExternalInput").ap()
    dwo = nc.dram_tensor("wob", [128, 8, 1024], F32, kind="ExternalInput").ap()
    dw1 = nc.dram_tensor("w1b", [32, 128, 8, 128], F32, kind="ExternalInput").ap()
    dw2 = nc.dram_tensor("w2b", [32, 128, 8, 128], F32, kind="ExternalInput").ap()
    dh1 = nc.dram_tensor("h1b", [32, 128, 32, 128], F32, kind="ExternalInput").ap()
    dh2 = nc.dram_tensor("h2b", [32, 128, 32, 128], F32, kind="ExternalInput").ap()
    dw3 = nc.dram_tensor("w3b", [32, 128, 1024], F32, kind="ExternalInput").ap()
    dsct = nc.dram_tensor("scT", [128, 128], F32, kind="ExternalInput").ap()
    dw3s = nc.dram_tensor("w3s", [1024], F32, kind="ExternalInput").ap()
    dtc = nc.dram_tensor("tc", [128, T], BF16, kind="ExternalInput").ap()
    dts = nc.dram_tensor("ts", [128, T], BF16, kind="ExternalInput").ap()
    dwqs = nc.dram_tensor("wqsb", [8, 128, 8, 128], F32,
                          kind="ExternalInput").ap()
    dwks = nc.dram_tensor("wksb", [8, 128, 8, 128], F32,
                          kind="ExternalInput").ap()
    dcm = nc.dram_tensor("cm01", [128, 896], BF16, kind="ExternalInput").ap()
    dmv = nc.dram_tensor("mval", [128, 8, 32], F32, kind="ExternalInput").ap()
    did = nc.dram_tensor("ident", [128, 128], BF16, kind="ExternalInput").ap()

    dout = nc.dram_tensor("out", [NQ, DM], F32, kind="ExternalOutput").ap()
    s_kt = nc.dram_tensor("kt_s", [H, 128, T], BF16, kind=SCR).ap()
    s_v = nc.dram_tensor("v_s", [H, T, 128], BF16, kind=SCR).ap()
    s_xp = nc.dram_tensor("xp_s", [NQ, DM], F32, kind=SCR).ap()
    s_x3 = nc.dram_tensor("x3_s", [8, 128, NQ], BF16, kind=SCR).ap()
    dqo = (nc.dram_tensor("qo_d", [128, 8, 4096], BF16,
                          kind="ExternalOutput").ap() if dbg else None)
    dx1 = (nc.dram_tensor("x1t_d", [128, 8, 512], BF16,
                          kind="ExternalOutput").ap() if dbg else None)
    s_w1 = nc.dram_tensor("w1_s", [32, 128, 8, 128], BF16, kind="Internal").ap()
    s_w2 = nc.dram_tensor("w2_s", [32, 128, 8, 128], BF16, kind="Internal").ap()
    s_h1 = nc.dram_tensor("h1_s", [32, 128, 32, 128], BF16, kind="Internal").ap()
    s_h2 = nc.dram_tensor("h2_s", [32, 128, 32, 128], BF16, kind="Internal").ap()
    s_w3 = nc.dram_tensor("w3_s", [32, 128, 1024], BF16, kind="Internal").ap()

    with ExitStack() as ctx:
        ecx = ctx.enter_context

        SB_BASE = 16512
        SB_LIMIT = 212832

        def at(name, off, shape, dtype):
            end = off + int(np.prod(shape[1:])) * mybir.dt.size(dtype)
            assert end <= SB_LIMIT, (name, end)
            return nc.alloc_sbuf_tensor_at(name, shape, dtype,
                                           offset=SB_BASE + off)

        ident = at("identt", 0, [128, 128], BF16)
        ones_c = at("ones_c", 256, [128, 1], BF16)
        ones_r = at("ones_r", 288, [1, 128], BF16)
        eps_t = at("eps_t", 544, [128, 1], F32)
        zero_t = at("zero_t", 576, [128, 1], F32)
        mval = at("mval_sb", 608, [128, 8, 32], F32)
        scT = at("scT_sb", 1632, [128, 128], F32)
        w3sb = at("w3sb", 2144, [128, 1024], F32)
        cm01 = at("cm01_sb", 6240, [128, 896], BF16)
        stt = []
        off = 8032
        for par in range(2):
            d = {}
            d["st6"] = at(f"st6_{par}", off, [128, 12], F32); off += 64
            d["ab"] = at(f"ab_{par}", off, [128, 4], F32); off += 32
            for nm in ("dm", "var", "m", "negm", "rstd", "bln"):
                d[nm] = at(f"{nm}_{par}", off, [128, 1], F32); off += 32
            stt.append(d)
        rrow = [at(f"rrow{p}", off + 1024 * p, [1, 512], BF16) for p in range(2)]
        off += 2048
        spc_a = at("spc_a", off, [128, 128], F32); off += 512
        spc_b = at("spc_b", off, [128, 128], F32); off += 512
        assert off <= 16384

        AR = 16384
        wqb = at("wqb_sb", AR, [128, 8, 8, 128], BF16)
        wkb = at("wkb_sb", AR + 16384, [128, 8, 8, 128], BF16)
        wqs = at("wqs_sb", AR + 32768, [128, 8, 8, 128], BF16)
        wks = at("wks_sb", AR + 49152, [128, 8, 8, 128], BF16)
        wvb = at("wvb_sb", AR + 65536, [128, 8, 1024], BF16)
        xb = [at(f"xb{p}", AR + 81920 + 8192 * p, [128, 4, 1024], BF16)
              for p in range(2)]
        x1s = [at(f"x1s{p}", AR + 98304 + 2048 * p, [128, 1024], BF16)
               for p in range(2)]
        x1t = at("x1t", AR + 102400, [128, 8, 512], BF16)
        krope = at("krope", AR + 110592, [128, 4, 512], BF16)
        vstage = at("vstage", AR + 114688, [128, 4, 1024], BF16)
        rp1 = at("rp1", AR + 122880, [128, 512], BF16)
        rp2 = at("rp2", AR + 123904, [128, 512], BF16)
        tcb = [at(f"tcb{p}", AR + 124928 + 1024 * p, [128, 512], BF16)
               for p in range(2)]
        tsb = [at(f"tsb{p}", AR + 126976 + 1024 * p, [128, 512], BF16)
               for p in range(2)]
        kbuf = [at(f"kbuf{p}", AR + 16384 * p, [128, T], BF16) for p in range(2)]
        vbuf = [at(f"vbuf{p}", AR + 32768 + 16384 * p, [128, 64, 128], BF16)
                for p in range(2)]
        pT = [at(f"pT{p}", AR + 65536 + 2048 * p, [128, 1024], BF16)
              for p in range(2)]
        bcs = at("bcs", AR + 69632, [128, 512], F32)
        P3 = AR + 73728
        wob = at("wob_sb", P3, [128, 8, 1024], BF16)
        xo = [at(f"xo{p}", P3 + 16384 + 4096 * p, [128, 1024], F32)
              for p in range(2)]
        xps = [at(f"xps{p}", P3 + 24576 + 4096 * p, [128, 1024], F32)
               for p in range(2)]
        x3s = [at(f"x3s{p}", P3 + 32768 + 2048 * p, [128, 1024], BF16)
               for p in range(2)]
        x3t = [at(f"x3t{p}", P3 + 36864 + 2048 * p, [128, 8, 128], BF16)
               for p in range(2)]
        QO = at("QO", SB_LIMIT - 65536, [128, 8, 4096], BF16)
        AR4 = AR - 192
        ha = at("ha", AR4, [128, 32, 1024], BF16)
        hb = at("hb", AR4 + 65536, [128, 32, 1024], BF16)
        x3tt = at("x3tt", AR4 + 131072, [128, 8, 1024], BF16)
        ws = [at(f"ws{p}", AR4 + 147456 + 8192 * p, [128, 4096], BF16)
              for p in range(3)]
        xpb = [at(f"xpb{p}", AR4 + 172032 + 4096 * p, [128, 1024], F32)
               for p in range(2)]
        outb = [at(f"outb{p}", AR4 + 180224 + 4096 * p, [128, 1024], F32)
                for p in range(2)]
        f4 = [at(f"f4{p}", AR4 + 188416 + 4096 * p, [128, 1024], F32)
              for p in range(2)]

        pA = ecx(nc.psum_tensor("pA", [128, 1024], F32))
        pB = ecx(nc.psum_tensor("pB", [128, 1024], F32))
        pC = ecx(nc.psum_tensor("pC", [128, 1024], F32))
        pD = ecx(nc.psum_tensor("pD", [128, 1024], F32))
        spair = [pA, pB]

        sem_names = [
            "c_ld", "prep", "x_ld", "sst", "apl", "tps", "tev", "pk", "rk",
            "pv2", "vev", "spk", "spv", "kvl", "smm", "msk", "expd", "spd",
            "rcp", "bce", "nrm", "xol", "pr3", "sv3", "t3d", "xpd", "spxp",
            "sv3b", "ap3", "tp3", "te3", "spx3", "wl4", "x3l", "xpl", "pe4",
            "rsd1", "rsd3", "rsd4", "tbl0", "tbl1", "xld0", "xld1", "kvl0", "kvl1",
            "spk0", "spk1", "spk2", "spk3", "spxp0", "spxp1", "spx30",
            "spx31", "xol0", "xol1", "xpl0", "xpl1", "outd0", "outd1",
            "wl40", "wl41", "wl42",
            "ev4", "f4d", "s4d", "o4d", "outd", "ini",
        ]
        SM = {n: ecx(nc.semaphore(n)) for n in sem_names}
        N_CLD = 11 * 16

        blk = ecx(nc.Block())

        def vsp(ve):
            return ve.tensor_copy(out=spc_b[:], in_=spc_a[:])

        def ssp(sc):
            return sc.activation(out=spc_b[:, 0:64], in_=spc_a[:, 0:64],
                                 func=AF.Copy)

        def stats_dve(ve, d, src_a, src_b, sem):
            ve.bn_stats(out=d["st6"][:, 0:6], in_=src_a)
            ve.bn_stats(out=d["st6"][:, 6:12], in_=src_b)
            vsp(ve)
            ve.bn_aggr(out=d["ab"][:, 0:2], in_=d["st6"][:, 0:12])
            vsp(ve)
            vsp(ve).then_inc(sem, 1)

        def rstd_scalar(sc, d):
            sc.activation(out=d["dm"][:], in_=d["ab"][:, 1:2], func=AF.Ln,
                          bias=eps_t[:, 0:1])
            ssp(sc)
            sc.activation(out=d["rstd"][:], in_=d["dm"][:], func=AF.Exp,
                          scale=-0.5)
            ssp(sc)
            return ssp(sc)

        def ln_apply_dve(ve, d, out_ap, in_ap):
            return ve.tensor_scalar(
                out=out_ap, in0=in_ap,
                scalar1=d["ab"][:, 0:1], scalar2=d["rstd"][:, 0:1],
                op0=ALU.subtract, op1=ALU.mult)

        def p3_tp(te, qc):
            te.wait_ge(SM["ap3"], qc + 1)
            if qc >= 2:
                te.wait_ge(SM["te3"], qc - 1)
            dst = pD.bitcast(BF16)
            half = qc % 2
            for fc in range(8):
                mm = te.transpose(
                    out=dst[:, half * 1024 + fc * 128:half * 1024 + (fc + 1) * 128],
                    in_=x3s[qc % 2][:, fc * 128:(fc + 1) * 128],
                    identity=ident[:])
            mm.then_inc(SM["tp3"], 1)

        def p3_te3(ve, qc):
            ve.wait_ge(SM["tp3"], qc + 1)
            if qc >= 2:
                ve.wait_ge(SM["spx30" if qc % 2 == 0 else "spx31"],
                           16 * (qc // 2))
            half = qc % 2
            ve.tensor_copy(
                out=x3t[qc % 2][:],
                in_=pD.bitcast(BF16)[:, half * 1024:(half + 1) * 1024]
                .rearrange("p (c d) -> p c d", c=8)).then_inc(SM["te3"], 1)

        @blk.gpsimd
        def _(gp):
            gp.dma_start(out=ident[:], in_=did).then_inc(SM["c_ld"], 16)
            gp.dma_start(out=cm01[:], in_=dcm).then_inc(SM["c_ld"], 16)
            gp.dma_start(out=mval[:], in_=dmv).then_inc(SM["c_ld"], 16)
            gp.dma_start(out=scT[:], in_=dsct).then_inc(SM["c_ld"], 16)
            gp.dma_start(out=w3sb[:],
                         in_=dw3s[None, :].partition_broadcast(128)
                         ).then_inc(SM["c_ld"], 16)
            gp.dma_start(out=wqb[:], in_=dwq.rearrange("o p i d -> p o i d")).then_inc(SM["c_ld"], 16)
            gp.dma_start(out=wkb[:], in_=dwk.rearrange("o p i d -> p o i d")).then_inc(SM["c_ld"], 16)
            gp.dma_start(out=wqs[:], in_=dwqs.rearrange("o p i d -> p o i d")).then_inc(SM["c_ld"], 16)
            gp.dma_start(out=wks[:], in_=dwks.rearrange("o p i d -> p o i d")).then_inc(SM["c_ld"], 16)
            gp.dma_start(out=wvb[:], in_=dwv.rearrange("i p m -> p i m")).then_inc(SM["c_ld"], 16)
            def load_x(t):
                gp.dma_start(
                    out=xb[t % 2][:],
                    in_=dx[t * 512:(t + 1) * 512, :].rearrange(
                        "(i p) m -> p i m", p=128),
                ).then_inc(SM["xld0" if t % 2 == 0 else "xld1"], 16)

            load_x(0)
            load_x(1)
            for t in range(NT):
                t0 = t * 512
                if t + 2 < NT:
                    gp.wait_ge(SM["apl"], 4 * (t + 1))
                    load_x(t + 2)
                for h in range(H):
                    gp.wait_ge(SM["rk"], P1_GCUM[t] + h + 1)
                    gk = t * 8 + h
                    gp.dma_start(out=s_kt[h, :, t0:t0 + 512],
                                 in_=krope[:, gk % 4, :]
                                 ).then_inc(SM["spk" + str(gk % 4)], 16)
                gp.wait_ge(SM["vev"], 8 * (t + 1))
                for h in range(H):
                    gp.dma_start(
                        out=s_v[h, t0:t0 + 512, :].rearrange(
                            "(i p) d -> p i d", p=128),
                        in_=vstage[:, :, h * 128:(h + 1) * 128],
                    ).then_inc(SM["spv"], 16)
                if dbg and t == 0:
                    gp.dma_start(out=dx1, in_=x1t[:]).then_inc(SM["s4d"], 16)
            gp.wait_ge(SM["apl"], 64)
            gp.wait_ge(SM["vev"], 8 * NT)
            gp.wait_ge(SM["rk"], P1_GCUM[NT])
            gp.dma_start(out=wob[:], in_=dwo).then_inc(SM["c_ld"], 16)
            gp.wait_ge(SM["nrm"], 8)
            gp.dma_start(out=s_w1, in_=dw1).then_inc(SM["prep"], 16)
            gp.dma_start(out=s_w2, in_=dw2).then_inc(SM["prep"], 16)
            gp.wait_ge(SM["nrm"], 16)
            gp.dma_start(out=s_h1, in_=dh1).then_inc(SM["prep"], 16)
            gp.wait_ge(SM["nrm"], 24)
            gp.dma_start(out=s_h2, in_=dh2).then_inc(SM["prep"], 16)
            gp.wait_ge(SM["nrm"], 32)
            gp.dma_start(out=s_w3, in_=dw3).then_inc(SM["prep"], 16)
            if dbg:
                gp.wait_ge(SM["nrm"], NSLOT)
                gp.dma_start(out=dqo, in_=QO[:]).then_inc(SM["s4d"], 16)

        @blk.sync
        def _(sy):
            for t in range(NT):
                if t >= 2:
                    sy.wait_ge(SM["rk"], P1_GCUM[t - 1])
                tb = "tbl0" if t % 2 == 0 else "tbl1"
                sy.dma_start(out=tcb[t % 2][:],
                             in_=dtc[:, t * 512:(t + 1) * 512]
                             ).then_inc(SM[tb], 16)
                sy.dma_start(out=tsb[t % 2][:],
                             in_=dts[:, t * 512:(t + 1) * 512]
                             ).then_inc(SM[tb], 16)
            for _r in range(4):
                sy.wait_ge(SM["spk" + str(_r)], 16 * 32)
            sy.wait_ge(SM["spv"], 16 * NT * H)
            for h in range(H):
                if h >= 2:
                    sy.wait_ge(SM["nrm"], 8 * (h - 1))
                kb, vb = kbuf[h % 2], vbuf[h % 2]
                kvs = "kvl0" if h % 2 == 0 else "kvl1"
                for i in range(8):
                    a, b = 4 * i * 128, 4 * (i + 1) * 128
                    sy.dma_start(out=kb[:, a:b],
                                 in_=s_kt[h, :, a:b]).then_inc(SM[kvs], 16)
                    sy.dma_start(
                        out=vb[:, 4 * i:4 * (i + 1), :],
                        in_=s_v[h, a:b, :].rearrange("(c p) d -> p c d", p=128),
                    ).then_inc(SM[kvs], 16)
                    if C2NEW[i]:
                        oc0 = 32 + (0 if i == 0 else 16)
                        ga, gb = oc0 * 128, (oc0 + 16) * 128
                        sy.dma_start(out=kb[:, ga:gb],
                                     in_=s_kt[h, :, ga:gb]
                                     ).then_inc(SM[kvs], 16)
                        for half in range(2):
                            c0 = oc0 + 8 * half
                            sy.dma_start(
                                out=vb[:, c0:c0 + 8, :],
                                in_=s_v[h, c0 * 128:(c0 + 8) * 128, :]
                                .rearrange("(c p) d -> p c d", p=128),
                            ).then_inc(SM[kvs], 16)
            for qc in range(32):
                if qc >= 2:
                    sy.wait_ge(SM["xpd"], qc - 1)
                sy.dma_start(out=xo[qc % 2][:],
                             in_=dx[qc * 128:(qc + 1) * 128, :]
                             ).then_inc(SM["xol0" if qc % 2 == 0 else "xol1"],
                                        16)
            sy.wait_ge(SM["nrm"], NSLOT)
            sy.wait_ge(SM["spx30"], 16 * 16)
            sy.wait_ge(SM["spx31"], 16 * 16)
            sy.wait_ge(SM["spxp0"], 16 * 16)
            sy.wait_ge(SM["spxp1"], 16 * 16)
            scr = {"w1": s_w1, "w2": s_w2, "h1": s_h1, "h2": s_h2}
            sy.wait_ge(SM["prep"], 16 * 5)
            lg = 0
            for t4 in range(4):
                if t4 >= 1:
                    sy.wait_ge(SM["pe4"], GR_TILE * (t4 - 1) + 64)
                sy.dma_start(
                    out=x3tt[:],
                    in_=s_x3[:, :, t4 * 1024:(t4 + 1) * 1024].rearrange(
                        "c p t -> p c t"),
                ).then_inc(SM["x3l"], 16)
                for li, (lname, o0, cnt, grp) in enumerate(P4_SCHED):
                    if lg >= 3:
                        if li >= 3:
                            pl, po, pc, pg = P4_SCHED[li - 3]
                            gate = GR_TILE * t4 + pg + pc
                        else:
                            pl, po, pc, pg = P4_SCHED[li - 3 + N_LOADS_TILE]
                            gate = GR_TILE * (t4 - 1) + pg + pc
                        sy.wait_ge(SM["pe4"], gate)
                    if lname == "w3":
                        src = s_w3[o0:o0 + cnt].rearrange("i p m -> p i m")
                        dst = ws[lg % 3][:, 0:cnt * 1024].rearrange(
                            "p (i m) -> p i m", i=cnt)
                    elif lname in ("w1", "w2"):
                        src = scr[lname][o0:o0 + cnt].rearrange(
                            "o p i d -> p o i d")
                        dst = ws[lg % 3][:, 0:cnt * 1024].rearrange(
                            "p (o i d) -> p o i d", o=cnt, i=8)
                    else:
                        src = scr[lname][o0].rearrange("p i d -> p i d")
                        dst = ws[lg % 3][:, 0:4096].rearrange(
                            "p (i d) -> p i d", i=32)
                    sy.dma_start(out=dst,
                                 in_=src).then_inc(SM["wl4" + str(lg % 3)], 16)
                    lg += 1
                    if lname == "w3" and o0 == 28:
                        pr = (grp - 128) // 32
                        for k in range(2):
                            qg = t4 * 8 + pr * 2 + k
                            if qg >= 2:
                                sy.wait_ge(SM["o4d"], qg - 1)
                            sy.dma_start(out=xpb[k][:],
                                         in_=s_xp[qg * 128:(qg + 1) * 128, :]
                                         ).then_inc(
                                SM["xpl0" if k == 0 else "xpl1"], 16)

        @blk.tensor
        def _(te):
            te.wait_ge(SM["c_ld"], N_CLD - 16)
            te.wait_ge(SM["ini"], 1)
            pkc = 0
            for t in range(NT):
                for i in range(4):
                    te.wait_ge(SM["apl"], 4 * t + i + 1)
                    if 4 * t + i >= 2:
                        te.wait_ge(SM["tev"], 4 * t + i - 1)
                    dst = pD.bitcast(BF16)
                    half = (4 * t + i) % 2
                    for fc in range(8):
                        mm = te.transpose(
                            out=dst[:, half * 1024 + fc * 128:
                                    half * 1024 + (fc + 1) * 128],
                            in_=x1s[i % 2][:, fc * 128:(fc + 1) * 128],
                            identity=ident[:])
                    mm.then_inc(SM["tps"], 1)
                te.wait_ge(SM["tev"], 4 * (t + 1))
                for (kind, h) in p1_gseq(t):
                    wb = wkb if kind == "K" else wqb
                    wbs = wks if kind == "K" else wqs
                    bank = pA if pkc % 2 == 0 else pB
                    if pkc >= 2:
                        te.wait_ge(SM["rk"], pkc - 1)
                    for ic in range(8):
                        te.matmul(bank[:, 0:512], wb[:, h, ic, :],
                                  x1t[:, ic, :],
                                  start=(ic == 0), stop=(ic == 7))
                        mm = te.matmul(bank[:, 512:1024], wbs[:, h, ic, :],
                                       x1t[:, ic, :],
                                       start=(ic == 0), stop=(ic == 7))
                    mm.then_inc(SM["pk"], 1)
                    pkc += 1
                for i2 in range(4):
                    for nh in range(2):
                        v = 8 * t + 2 * i2 + nh
                        if v >= 2:
                            te.wait_ge(SM["vev"], v - 1)
                        for ic in range(8):
                            mm = te.matmul(
                                pC[:, nh * 512:(nh + 1) * 512],
                                x1t[:, ic, i2 * 128:(i2 + 1) * 128],
                                wvb[:, ic, nh * 512:(nh + 1) * 512],
                                start=(ic == 0), stop=(ic == 7))
                        mm.then_inc(SM["pv2"], 1)
            def emit_S(d):
                if d["first"]:
                    te.wait_ge(SM["kvl0" if d["h"] % 2 == 0 else "kvl1"],
                               16 * 22 * (d["h"] // 2 + 1))
                gp_ = d["gp"]
                if gp_ >= 2:
                    te.wait_ge(SM["expd"], gp_ - 1)
                sp = spair[gp_ % 2]
                kb = kbuf[d["h"] % 2]
                qslice = QO[:, d["h"], d["i"] * 512:(d["i"] + 1) * 512]
                for k in range(2):
                    c = d["c0"] + k
                    mm = te.matmul(sp[:, k * 512:(k + 1) * 512],
                                   kb[:, c * 128:(c + 1) * 128], qslice,
                                   start=True, stop=True)
                mm.then_inc(SM["smm"], 1)

            def emit_SP(d):
                gp_ = d["gp"]
                sm = P2_SLOTS[(d["h"], d["i"])]
                if d["masked"]:
                    te.wait_ge(SM["msk"], d["msk_after"])
                else:
                    te.wait_ge(SM["expd"], gp_ + 1)
                if d["first"] and sm["n_slot"] >= 2:
                    te.wait_ge(SM["rcp"], sm["n_slot"] - 1)
                    te.wait_ge(SM["nrm"], sm["n_slot"] - 1)
                vb = vbuf[d["h"] % 2]
                pt = pT[gp_ % 2]
                for k in range(2):
                    c = d["c0"] + k
                    st = d["first"] and k == 0
                    sp_ = d["last_of_slot"] and k == 1
                    te.matmul(pC[0:1, 0:512], ones_c[:],
                              pt[:, k * 512:(k + 1) * 512], start=st, stop=sp_)
                    mm = te.matmul(pD[:, 0:512], vb[:, c, :],
                                   pt[:, k * 512:(k + 1) * 512],
                                   start=st, stop=sp_)
                mm.then_inc(SM["spd"], 1)

            def emit_bcast(d):
                sm = P2_SLOTS[(d["h"], d["i"])]
                te.wait_ge(SM["rcp"], sm["n_slot"])
                te.matmul(pC[:, 512:1024], ones_r[:],
                          rrow[(sm["n_slot"] - 1) % 2][:],
                          start=True, stop=True).then_inc(SM["bce"], 1)

            prev = None
            pending_bcast = None
            for d in P2_PAIRS:
                if prev is not None and prev["last_of_slot"]:
                    emit_SP(prev)
                    pending_bcast = prev
                    prev = None
                emit_S(d)
                if pending_bcast is not None:
                    emit_bcast(pending_bcast)
                    pending_bcast = None
                if prev is not None:
                    emit_SP(prev)
                prev = d
            emit_SP(prev)
            emit_bcast(prev)
            te.wait_ge(SM["nrm"], NSLOT)
            te.wait_ge(SM["c_ld"], N_CLD)
            for qc in range(32):
                dst = pA if qc % 2 == 0 else pB
                if qc >= 2:
                    te.wait_ge(SM["t3d"], qc - 1)
                for h in range(H):
                    for nh in range(2):
                        mm = te.matmul(dst[:, nh * 512:(nh + 1) * 512],
                                       QO[:, h, qc * 128:(qc + 1) * 128],
                                       wob[:, h, nh * 512:(nh + 1) * 512],
                                       start=(h == 0), stop=(h == 7))
                mm.then_inc(SM["pr3"], 1)
                if qc >= 1:
                    p3_tp(te, qc - 1)
            p3_tp(te, 31)
            ev4_base = 0
            lgc = 0
            for t4 in range(4):
                te.wait_ge(SM["x3l"], 16 * (t4 + 1))
                if t4 >= 1:
                    te.wait_ge(SM["f4d"], 8 * t4)
                ev4_seen = 0
                for (lname, o0, cnt, grp) in P4_SCHED:
                    te.wait_ge(SM["wl4" + str(lgc % 3)], 16 * (lgc // 3 + 1))
                    lgc += 1
                    if lname == "w3":
                        pr = (grp - 128) // 32
                        pe_pair = [pA, pB] if pr % 2 == 0 else [pC, pD]
                        if o0 == 0:
                            if pr == 0:
                                te.wait_ge(SM["ev4"], ev4_base + 128)
                            if pr >= 2:
                                te.wait_ge(SM["f4d"],
                                           8 * t4 + (pr - 2) * 2 + 2)
                        for j in range(cnt):
                            ic = o0 + j
                            for k in range(2):
                                i3 = pr * 2 + k
                                for nh in range(2):
                                    mm = te.matmul(
                                        pe_pair[k][:, nh * 512:(nh + 1) * 512],
                                        hb[:, ic, i3 * 128:(i3 + 1) * 128],
                                        ws[(lgc - 1) % 3]
                                        [:, (j * 8 + nh * 4) * 128:
                                         (j * 8 + nh * 4 + 4) * 128]
                                        if False else
                                        ws[(lgc - 1) % 3]
                                        [:, j * 1024 + nh * 512:
                                         j * 1024 + (nh + 1) * 512],
                                        start=(ic == 0), stop=(ic == 31))
                            mm.then_inc(SM["pe4"], 1)
                        continue
                    if lname == "h1" and o0 == 0:
                        te.wait_ge(SM["ev4"], ev4_base + 64)
                    if lname == "h2" and o0 == 0:
                        te.wait_ge(SM["ev4"], ev4_base + 96)
                    src = x3tt if lname in ("w1", "w2") else (
                        hb if lname == "h1" else ha)
                    n_ic = 8 if lname in ("w1", "w2") else 32
                    for j in range(cnt):
                        oc = o0 + j
                        dstp = pA if oc % 2 == 0 else pB
                        if ev4_seen >= 2:
                            te.wait_ge(SM["ev4"], ev4_base + ev4_seen - 1)
                        for ic in range(n_ic):
                            lhs = ws[(lgc - 1) % 3][
                                :, (j * n_ic + ic) * 128:
                                (j * n_ic + ic + 1) * 128]
                            te.matmul(dstp[:, 0:512], lhs, src[:, ic, 0:512],
                                      start=(ic == 0), stop=(ic == n_ic - 1))
                            mm = te.matmul(dstp[:, 512:1024], lhs,
                                           src[:, ic, 512:1024],
                                           start=(ic == 0),
                                           stop=(ic == n_ic - 1))
                        mm.then_inc(SM["pe4"], 1)
                        ev4_seen += 1
                ev4_base += 128

        @blk.vector
        def _(ve):
            ve.memset(ones_c[:], 1.0)
            ve.memset(ones_r[:], 1.0)
            ve.memset(eps_t[:], EPS)
            ve.memset(zero_t[:], 0.0)
            ve.memset(spc_a[:], 0.0)
            ve.memset(spc_b[:], 0.0)
            vsp(ve)
            vsp(ve).then_inc(SM["ini"], 1)
            ve.wait_ge(SM["c_ld"], N_CLD - 16)
            def p1_stats(ti):
                t, i = ti // 4, ti % 4
                d = stt[i % 2]
                ve.wait_ge(SM["xld0" if t % 2 == 0 else "xld1"],
                           16 * (t // 2 + 1))
                if ti >= 2:
                    ve.wait_ge(SM["apl"], ti - 1)
                src = xb[t % 2][:, i, :]
                stats_dve(ve, d, src[0:128, 0:512], src[0:128, 512:1024],
                          SM["sst"])

            p1_stats(0)
            for t in range(NT):
                for i in range(4):
                    d = stt[i % 2]
                    if 4 * t + i + 1 < 64:
                        p1_stats(4 * t + i + 1)
                    src = xb[t % 2][:, i, :]
                    ve.wait_ge(SM["rsd1"], 4 * t + i + 1)
                    if 4 * t + i >= 2:
                        ve.wait_ge(SM["tps"], 4 * t + i - 1)
                    ln_apply_dve(ve, d, x1s[i % 2][:], src
                                 ).then_inc(SM["apl"], 1)
                    if i >= 1:
                        ti = 4 * t + i - 1
                        ve.wait_ge(SM["tps"], ti + 1)
                        half = ti % 2
                        ve.tensor_copy(
                            out=x1t[:, :, (ti % 4) * 128:(ti % 4) * 128 + 128],
                            in_=pD.bitcast(BF16)[:, half * 1024:
                                                 (half + 1) * 1024]
                            .rearrange("p (c d) -> p c d", c=8)
                        ).then_inc(SM["tev"], 1)
                ti = 4 * t + 3
                ve.wait_ge(SM["tps"], ti + 1)
                half = ti % 2
                ve.tensor_copy(
                    out=x1t[:, :, 384:512],
                    in_=pD.bitcast(BF16)[:, half * 1024:(half + 1) * 1024]
                    .rearrange("p (c d) -> p c d", c=8)).then_inc(SM["tev"], 1)
                ve.wait_ge(SM["tbl0" if t % 2 == 0 else "tbl1"],
                           32 * (t // 2 + 1))
                for gi, (kind, h) in enumerate(p1_gseq(t)):
                    g = P1_GCUM[t] + gi
                    ve.wait_ge(SM["pk"], g + 1)
                    if kind == "K":
                        gk = t * 8 + h
                        if gk >= 4:
                            ve.wait_ge(SM["spk" + str(gk % 4)],
                                       16 * (gk // 4))
                    bank = pA if g % 2 == 0 else pB
                    tt0 = t * 512
                    ve.tensor_tensor(out=rp1[:], in0=bank[:, 0:512],
                                     in1=tcb[t % 2][:], op=ALU.mult)
                    ve.tensor_tensor(out=rp2[:], in0=bank[:, 512:1024],
                                     in1=tsb[t % 2][:], op=ALU.mult)
                    if kind == "K":
                        dst = krope[:, (t * 8 + h) % 4, :]
                    else:
                        dst = QO[:, h, tt0:tt0 + 512]
                    ve.tensor_tensor(out=dst, in0=rp1[:], in1=rp2[:],
                                     op=ALU.add).then_inc(SM["rk"], 1)
            for d in P2_PAIRS:
                gp_ = d["gp"]
                if d["masked"]:
                    ve.wait_ge(SM["expd"], gp_ + 1)
                    pt = pT[gp_ % 2]
                    n_own = 4 * (d["i"] + 1)
                    for k in range(2):
                        c = d["c0"] + k
                        r = c - (n_own - 4)
                        s = 384 - 128 * r
                        mm = ve.tensor_tensor(
                            out=pt[:, k * 512:(k + 1) * 512],
                            in0=pt[:, k * 512:(k + 1) * 512],
                            in1=cm01[:, s:s + 512], op=ALU.mult)
                    mm.then_inc(SM["msk"], 1)
                if d["last_of_slot"]:
                    sm = P2_SLOTS[(d["h"], d["i"])]
                    ve.wait_ge(SM["spd"], sm["last_gp"] + 1)
                    with nc.allow_low_precision(
                            reason="softmax 1/sum as bf16 matmul operand"):
                        ve.reciprocal(out=rrow[(sm["n_slot"] - 1) % 2][:],
                                      in_=pC[0:1, 0:512])
                    vsp(ve).then_inc(SM["rcp"], 1)
                    ve.wait_ge(SM["bce"], sm["n_slot"])
                    ve.tensor_copy(out=bcs[:], in_=pC[:, 512:1024])
                    ve.tensor_tensor(
                        out=QO[:, d["h"], d["i"] * 512:(d["i"] + 1) * 512],
                        in0=pD[:, 0:512], in1=bcs[:], op=ALU.mult
                    ).then_inc(SM["nrm"], 1)
            for qc in range(32):
                par = qc % 2
                d = stt[par]
                srcp = pA if par == 0 else pB
                ve.wait_ge(SM["pr3"], qc + 1)
                stats_dve(ve, d, srcp[:, 0:512], srcp[:, 512:1024], SM["sv3"])
                ve.wait_ge(SM["rsd3"], 2 * qc + 1)
                if qc >= 2:
                    ve.wait_ge(SM["spxp0" if qc % 2 == 0 else "spxp1"],
                               16 * (qc // 2))
                ln_apply_dve(ve, d, xps[par][:], srcp[:]
                             ).then_inc(SM["t3d"], 1)
                ve.wait_ge(SM["xol0" if qc % 2 == 0 else "xol1"],
                           16 * (qc // 2 + 1))
                ve.tensor_add(xps[par][:], xps[par][:], xo[par][:]
                              ).then_inc(SM["xpd"], 1)
                stats_dve(ve, d, xps[par][:, 0:512], xps[par][:, 512:1024],
                          SM["sv3b"])
                ve.wait_ge(SM["rsd3"], 2 * qc + 2)
                if qc >= 2:
                    ve.wait_ge(SM["tp3"], qc - 1)
                ln_apply_dve(ve, d, x3s[par][:], xps[par][:]
                             ).then_inc(SM["ap3"], 1)
                if qc >= 1:
                    p3_te3(ve, qc - 1)
                if qc == 31:
                    p3_te3(ve, 31)
            pe4c = 0
            for t4 in range(4):
                for (lname, n_oc, n_ic) in P4_LAYERS:
                    for oc in range(n_oc):
                        if lname == "w2":
                            ve.wait_ge(SM["pe4"], pe4c + 1)
                            srcp = pA if oc % 2 == 0 else pB
                            ve.scalar_tensor_tensor(
                                out=hb[:, oc, :], in0=srcp[:],
                                scalar=scT[:, 32 + oc:33 + oc],
                                in1=ha[:, oc, :], op0=ALU.mult, op1=ALU.mult
                            ).then_inc(SM["ev4"], 1)
                        pe4c += 1
                pe4c += 128
                for i3 in range(8):
                    qg = t4 * 8 + i3
                    par = i3 % 2
                    d = stt[par]
                    pr, k = i3 // 2, i3 % 2
                    srcp = ([pA, pB] if pr % 2 == 0 else [pC, pD])[k]
                    ve.wait_ge(SM["pe4"], GR_TILE * t4 + 128 + 32 * (pr + 1))
                    ve.tensor_tensor(out=f4[par][:], in0=srcp[:], in1=w3sb[:],
                                     op=ALU.mult).then_inc(SM["f4d"], 1)
                    stats_dve(ve, d, f4[par][:, 0:512], f4[par][:, 512:1024],
                              SM["sv3"])
                    ve.wait_ge(SM["rsd4"], qg + 1)
                    if qg >= 2:
                        ve.wait_ge(SM["outd0" if qg % 2 == 0 else "outd1"],
                                   16 * (qg // 2))
                    ln_apply_dve(ve, d, outb[par][:], f4[par][:])
                    ve.wait_ge(SM["xpl0" if i3 % 2 == 0 else "xpl1"],
                               16 * (qg // 2 + 1))
                    ve.tensor_add(outb[par][:], outb[par][:], xpb[par][:]
                                  ).then_inc(SM["o4d"], 1)

        @blk.scalar
        def _(sc):
            sc.wait_ge(SM["ini"], 1)
            for t in range(NT):
                for i in range(4):
                    d = stt[i % 2]
                    sc.wait_ge(SM["sst"], 4 * t + i + 1)
                    rstd_scalar(sc, d).then_inc(SM["rsd1"], 1)
                for i2 in range(4):
                    for nh in range(2):
                        v = 8 * t + 2 * i2 + nh
                        sc.wait_ge(SM["pv2"], v + 1)
                        if t >= 1 and i2 == 0 and nh == 0:
                            sc.wait_ge(SM["spv"], 16 * 8 * t)
                        sc.activation(
                            out=vstage[:, i2, nh * 512:(nh + 1) * 512],
                            in_=pC[:, nh * 512:(nh + 1) * 512],
                            func=AF.Copy).then_inc(SM["vev"], 1)
            for d in P2_PAIRS:
                gp_ = d["gp"]
                sc.wait_ge(SM["smm"], gp_ + 1)
                if gp_ >= 2:
                    sc.wait_ge(SM["spd"], gp_ - 1)
                bias = mval[:, d["i"], d["pidx"]:d["pidx"] + 1]
                sc.activation(out=pT[gp_ % 2][:], in_=spair[gp_ % 2][:],
                              func=AF.Exp, bias=bias).then_inc(SM["expd"], 1)
            for qc in range(32):
                par = qc % 2
                d = stt[par]
                sc.wait_ge(SM["sv3"], qc + 1)
                rstd_scalar(sc, d).then_inc(SM["rsd3"], 1)
                sc.wait_ge(SM["sv3b"], qc + 1)
                rstd_scalar(sc, d).then_inc(SM["rsd3"], 1)
                sc.wait_ge(SM["xpd"], qc + 1)
                sc.dma_start(out=s_xp[qc * 128:(qc + 1) * 128, :],
                             in_=xps[par][:]
                             ).then_inc(SM["spxp0" if par == 0 else "spxp1"],
                                        16)
                if qc >= 1:
                    sc.wait_ge(SM["te3"], qc)
                    sc.dma_start(out=s_x3[:, :, (qc - 1) * 128:qc * 128]
                                 .rearrange("c p d -> p c d"),
                                 in_=x3t[(qc - 1) % 2][:]
                                 ).then_inc(
                        SM["spx30" if (qc - 1) % 2 == 0 else "spx31"], 16)
            sc.wait_ge(SM["te3"], 32)
            sc.dma_start(out=s_x3[:, :, 31 * 128:32 * 128]
                         .rearrange("c p d -> p c d"),
                         in_=x3t[31 % 2][:]).then_inc(SM["spx31"], 16)
            pe4c = 0
            for t4 in range(4):
                for (lname, n_oc, n_ic) in P4_LAYERS:
                    for oc in range(n_oc):
                        if lname != "w2":
                            sc.wait_ge(SM["pe4"], pe4c + 1)
                            srcp = pA if oc % 2 == 0 else pB
                            dst = ha[:, oc, :] if lname in ("w1", "h1") \
                                else hb[:, oc, :]
                            fn = AF.Silu if lname == "w1" else AF.Gelu
                            col = {"w1": 0, "h1": 64, "h2": 96}[lname] + oc
                            sc.activation(out=dst, in_=srcp[:], func=fn,
                                          scale=scT[:, col:col + 1]
                                          ).then_inc(SM["ev4"], 1)
                        pe4c += 1
                pe4c += 128
                for i3 in range(8):
                    qg = t4 * 8 + i3
                    par = i3 % 2
                    d = stt[par]
                    sc.wait_ge(SM["sv3"], 32 + qg + 1)
                    rstd_scalar(sc, d).then_inc(SM["rsd4"], 1)
                    sc.wait_ge(SM["o4d"], qg + 1)
                    sc.dma_start(out=dout[qg * 128:(qg + 1) * 128, :],
                                 in_=outb[par][:]
                                 ).then_inc(
                        SM["outd0" if par == 0 else "outd1"], 16)
            sc.wait_ge(SM["outd0"], 16 * 16)
            sc.wait_ge(SM["outd1"], 16 * 16)

    return nc


def _make_tables(w):
    own, oth = _own_oth(w)
    perm = np.concatenate([own, oth])
    pos = perm.astype(np.float64)
    half = 64
    freqs = 10000.0 ** (-np.arange(half, dtype=np.float64) / half)
    rad = pos[None, :] * freqs[:, None]
    cs = np.cos(rad) * MULT
    sn = np.sin(rad) * MULT
    tc = np.concatenate([cs, cs], axis=0).astype(ml_dtypes.bfloat16)
    ts = np.concatenate([-sn, sn], axis=0).astype(ml_dtypes.bfloat16)
    return perm, tc, ts


def _make_masks(w):
    own, oth = _own_oth(w)
    mv = np.zeros((8, 32), dtype=np.float32)
    for i in range(8):
        for (kind, c0, pidx, masked) in p2_slot_pairs(i):
            if kind == "oth":
                mv[i, pidx] = NEG if oth[(c0 - 32) * 128] > own[i * 512] else 0.0
    mval_b = np.ascontiguousarray(
        np.broadcast_to(mv[None], (128, 8, 32)), dtype=np.float32)
    jj = np.arange(896)[None, :]
    pp = np.arange(128)[:, None]
    cm = (pp <= jj - 384).astype(np.float32).astype(ml_dtypes.bfloat16)
    return mval_b, cm


_PROG = None


def kernel(L0_attn_x, L0_attn_wq, L0_attn_wk, L0_attn_wv, L0_attn_wo,
           L0_ffn_w1_w, L0_ffn_w1_s, L0_ffn_w2_w, L0_ffn_w2_s,
           L0_ffn_h1_w, L0_ffn_h1_s, L0_ffn_h2_w, L0_ffn_h2_s,
           L0_ffn_w3_w, L0_ffn_w3_s):
    global _PROG
    x = np.asarray(L0_attn_x, dtype=np.float32)
    wq = np.asarray(L0_attn_wq, dtype=np.float32)
    wk = np.asarray(L0_attn_wk, dtype=np.float32)
    wv = np.asarray(L0_attn_wv, dtype=np.float32)
    wo = np.asarray(L0_attn_wo, dtype=np.float32)

    def blk_qk(wm):
        wt = np.ascontiguousarray(wm.transpose(1, 0, 2).reshape(1024, 1024))
        return np.ascontiguousarray(
            wt.reshape(8, 128, 8, 128).transpose(2, 1, 0, 3))

    wqb, wkb = blk_qk(wq), blk_qk(wk)
    swap = lambda wm: np.concatenate([wm[:, :, 64:], wm[:, :, :64]], axis=2)
    wqsb, wksb = blk_qk(swap(wq)), blk_qk(swap(wk))
    wvb = np.ascontiguousarray(
        wv.transpose(1, 0, 2).reshape(1024, 1024).reshape(8, 128, 1024))
    wob = np.ascontiguousarray(wo.transpose(1, 0, 2))

    def blk_ffn(wm, n_oc, n_ic):
        return np.ascontiguousarray(
            wm.reshape(n_oc, 128, n_ic, 128).transpose(0, 3, 2, 1))

    w1b = blk_ffn(np.asarray(L0_ffn_w1_w, np.float32), 32, 8)
    w2b = blk_ffn(np.asarray(L0_ffn_w2_w, np.float32), 32, 8)
    h1b = blk_ffn(np.asarray(L0_ffn_h1_w, np.float32), 32, 32)
    h2b = blk_ffn(np.asarray(L0_ffn_h2_w, np.float32), 32, 32)
    w3b = np.ascontiguousarray(
        np.asarray(L0_ffn_w3_w, np.float32).T.reshape(32, 128, 1024))
    sc4 = np.concatenate([np.asarray(s, np.float32).reshape(32, 128)
                          for s in (L0_ffn_w1_s, L0_ffn_w2_s,
                                    L0_ffn_h1_s, L0_ffn_h2_s)])
    scT = np.ascontiguousarray(sc4.T)
    w3s = np.ascontiguousarray(np.asarray(L0_ffn_w3_s, np.float32))
    ident = np.eye(128, dtype=ml_dtypes.bfloat16)

    percore = {}
    for w in range(2):
        perm, tca, tsa = _make_tables(w)
        mval_b, cm = _make_masks(w)
        percore[w] = (perm, tca, tsa, mval_b, cm)

    if _PROG is None:
        _PROG = build_program()
    nc = _PROG

    in_maps = []
    for c in range(8):
        b, w = c // 2, c % 2
        perm, tca, tsa, mval_b, cm = percore[w]
        in_maps.append({
            "x": np.ascontiguousarray(x[b][perm]),
            "wqb": wqb, "wkb": wkb, "wqsb": wqsb, "wksb": wksb,
            "wvb": wvb, "wob": wob,
            "w1b": w1b, "w2b": w2b, "h1b": h1b, "h2b": h2b, "w3b": w3b,
            "scT": scT, "w3s": w3s, "tc": tca, "ts": tsa,
            "cm01": cm, "mval": mval_b, "ident": ident,
        })
    res = run_bass_kernel_spmd(nc, in_maps, core_ids=list(range(8)))
    out = np.empty((B, T, DM), dtype=np.float32)
    for c in range(8):
        b, w = c // 2, c % 2
        own, _ = _own_oth(w)
        out[b][own] = res.results[c]["out"]
    return out
